# version 20
# baseline (speedup 1.0000x reference)
import contextlib
import numpy as np

import concourse.bass as bass
import concourse.mybir as mybir
from concourse.bass_utils import run_bass_kernel_spmd

F32 = mybir.dt.float32
BF16 = mybir.dt.bfloat16
I32 = mybir.dt.int32
AF = mybir.ActivationFunctionType
ALU = mybir.AluOpType
AX = mybir.AxisListType

B_, SEQ, D = 4, 4096, 1024
T_OWN = 2048
T_TOT = 4096
HID = 4096
NPAIR = 8
NQB = 4
QB = 512
NKC = 32
NSLAB = 16
EPS_LN = 1e-5
EPS_Q = 1e-5


class Sem:
    def __init__(self, nc, es, name):
        self.h = es.enter_context(nc.semaphore(name))
        self.name = name
        self.n = 0
        self.maxw = 0

    def inc(self, instr, k=1):
        instr.then_inc(self.h, k)
        self.n += k
        return self.n

    def wait(self, eng, val):
        assert val <= self.n, f"{self.name}: forward wait {val} > emitted {self.n}"
        if val <= 0:
            return
        self.maxw = max(self.maxw, val)
        eng.wait_ge(self.h, val)

    def wait_planned(self, eng, val):
        if val <= 0:
            return
        self.maxw = max(self.maxw, val)
        eng.wait_ge(self.h, val)

    def wait_all(self, eng):
        self.wait(eng, self.n)


def build_program(debug=False):
    nc = bass.Bass("TRN2", target_bir_lowering=False, debug=False)

    x_own = nc.dram_tensor("x_own", [T_OWN, D], F32, kind="ExternalInput").ap()
    x_oth = nc.dram_tensor("x_oth", [T_OWN, D], F32, kind="ExternalInput").ap()
    w_qkv = nc.dram_tensor("w_qkv", [D, 3 * D], F32, kind="ExternalInput").ap()
    w_out = nc.dram_tensor("w_out", [D, D], F32, kind="ExternalInput").ap()
    b_out = nc.dram_tensor("b_out", [D], F32, kind="ExternalInput").ap()
    w1 = nc.dram_tensor("w1", [HID, D], F32, kind="ExternalInput").ap()
    w2 = nc.dram_tensor("w2", [D, HID], F32, kind="ExternalInput").ap()
    out = nc.dram_tensor("out", [T_OWN, D], F32, kind="ExternalOutput").ap()

    xbf_d = nc.dram_tensor("xbf_d", [T_TOT, D], BF16, kind="Internal").ap()
    dbg_kind = "ExternalOutput" if debug else "Internal"
    w1t_d = nc.dram_tensor("w1t_d", [HID, D], BF16, kind=dbg_kind).ap()
    w2t_d = nc.dram_tensor("w2t_d", [D, HID], BF16, kind=dbg_kind).ap()
    ho_d = (nc.dram_tensor("ho_d", [128, 8 * T_OWN], BF16,
                           kind="ExternalOutput").ap() if debug else None)
    vec_d = (nc.dram_tensor("vec_d", [128, 96], F32,
                            kind="ExternalOutput").ap() if debug else None)
    x1_d = nc.dram_tensor("x1_d", [T_OWN, D], F32,
                          kind="ExternalOutput" if debug else "Internal").ap()

    m_sc = {}
    m_pv = {}

    es = contextlib.ExitStack()
    s_att2a = contextlib.ExitStack()
    s_att = contextlib.ExitStack()
    s_mid = contextlib.ExitStack()
    s_2a = contextlib.ExitStack()
    s_2b = contextlib.ExitStack()

    with es:
        def SB(stack, name, shape, dt):
            return stack.enter_context(nc.sbuf_tensor(name, shape, dt))

        def PS(stack, name, shape, dt):
            return stack.enter_context(nc.psum_tensor(name, shape, dt))

        ident = SB(es, "ident", [128, 128], BF16)
        ones_bf = SB(es, "ones_bf", [128, 2], BF16)
        ones_fc = SB(es, "ones_fc", [128, 1], F32)
        ones_f = SB(es, "ones_f", [1, 128], F32)
        negones_f = SB(es, "negones_f", [1, 128], F32)
        onesrow_bf = SB(es, "onesrow_bf", [1, 128], BF16)
        b_out_bf = SB(es, "b_out_bf", [1, D], BF16)
        vec = SB(es, "vec", [128, 32], F32)
        vec2 = SB(es, "vec2", [128, 32], F32)
        sml = SB(es, "sml", [1, 64], F32)
        acc_w = SB(es, "acc_w", [128, 4], F32)

        sem_g = Sem(nc, es, "sem_g")
        sem_s = Sem(nc, es, "sem_s")
        sem_o = Sem(nc, es, "sem_o")
        sem_xb = Sem(nc, es, "sem_xb")
        sem_xT = Sem(nc, es, "sem_xT")
        sem_wl = [Sem(nc, es, f"sem_wl{i}") for i in range(2)]
        sem_ts = [Sem(nc, es, f"sem_ts{i}") for i in range(2)]
        sem_wT = Sem(nc, es, "sem_wT")
        sem_xr = [Sem(nc, es, f"sem_xr{i}") for i in range(3)]
        sem_x1s = [Sem(nc, es, f"sem_x1s{i}") for i in range(2)]
        sem_y = [Sem(nc, es, f"sem_y{i}") for i in range(3)]
        sem_op = [Sem(nc, es, f"sem_op{i}") for i in range(2)]
        sem_wq = [Sem(nc, es, f"sem_wq{i}") for i in range(2)]
        sem_kv = [Sem(nc, es, f"sem_kv{i}") for i in range(2)]
        pe = Sem(nc, es, "pe")
        act = Sem(nc, es, "act")
        dve = Sem(nc, es, "dve")
        gp = Sem(nc, es, "gp")

        block = es.enter_context(nc.Block())
        PE = block.tensor
        ACTE = block.scalar
        DVE = block.vector
        GPS = block.gpsimd
        SYN = block.sync

        headoutT = s_att2a.enter_context(
            nc.sbuf_tensor("headoutT", [128, 8 * T_OWN], BF16, side="right"))
        xT = SB(s_att, "xT", [128, 8 * T_TOT], BF16)
        QT = SB(s_att, "QT", [128, 8 * T_OWN], BF16)
        k_pair = [SB(s_att, f"k_pair{i}", [128, T_TOT], BF16) for i in range(2)]
        v_aug = [SB(s_att, f"v_aug{i}", [128, 32 * 130 + 64], BF16) for i in range(2)]
        vfm = [SB(s_att, f"vfm{i}", [128, 512], BF16) for i in range(2)]
        probs = [SB(s_att, f"probs{i}", [128, 1024], BF16) for i in range(2)]
        sqb = [SB(s_att, f"sqb{i}", [128, 512], BF16) for i in range(2)]
        wkv = [SB(s_att, f"wkv{i}", [128, 8 * 256], BF16) for i in range(2)]
        wqm = [SB(s_att, f"wqm{i}", [128, 8 * 128], BF16) for i in range(2)]
        rec = SB(s_att, "rec", [64, 1024], F32)
        den2 = SB(s_att, "den2", [1, 1024], BF16)
        raw = [SB(s_att, f"raw{i}", [128, 512], BF16) for i in range(4)]
        wstage = SB(s_att, "wstage", [128, 1024], F32)
        tern_a = SB(s_att, "tern_a", [128, 1024], BF16)
        tern_b = SB(s_att, "tern_b", [128, 1024], BF16)
        tern_c = SB(s_att, "tern_c", [128, 1024], BF16)

        ps_s = [PS(s_att, f"ps_s{i}", [128, 1024], F32) for i in range(2)]
        ps_oA = PS(s_att, "ps_oA", [128, 512], F32)
        ps_oB = PS(s_att, "ps_oB", [128, 512], F32)
        ps_x = PS(s_att, "ps_x", [128, 512], F32)
        ps_t = PS(s_att, "ps_t", [128, 1024], BF16)

        psx_free = [(dve, 0)]

        def vsync(v, ins):
            dve.inc(ins)
            dve.wait(v, dve.n)

        def asyncb(sc, ins):
            act.inc(ins)
            act.wait_planned(sc, act.n)

        def psx_acquire(eng):
            s, v = psx_free[0]
            s.wait(eng, v)

        pst_free = [(dve, 0)]

        def _(g):
            g.memset(ident[:, :], 0.0)
            g.affine_select(out=ident[:, :], in_=ident[:, :],
                            compare_op=ALU.not_equal, fill=1.0, base=0,
                            pattern=[[-1, 128]], channel_multiplier=1)
            g.memset(ones_bf[:, :], 1.0)
            g.memset(ones_bf[64:128, 1:2], 0.0)
            ins = g.memset(onesrow_bf[:, :], 1.0)
            gp.inc(ins)
            g.dma_start(out=xbf_d[0:T_OWN, :], in_=x_own[:, :]).then_inc(sem_xb.h, 16)
            g.dma_start(out=xbf_d[T_OWN:T_TOT, :], in_=x_oth[:, :]).then_inc(sem_xb.h, 16)
            g.dma_start(out=b_out_bf[0:1, :], in_=b_out[None, :]).then_inc(sem_xb.h, 16)
        GPS(_)
        sem_xb.n += 48
        m_ident = gp.n

        def _(v):
            v.memset(ones_fc[:, :], 1.0)
            v.memset(ones_f[:, :], 1.0)
            v.memset(negones_f[:, :], -1.0)
            v.memset(acc_w[:, :], 0.0)
            v.memset(sml[:, :], 0.0)
            v.memset(vec2[:, :], 0.0)
            v.memset(vec[:, :], 0.0)
            for i in range(2):
                for kt in range(32):
                    v.memset(v_aug[i][:, kt * 130 + 64:kt * 130 + 65], 1.0)
                    v.memset(v_aug[i][:, kt * 130 + 129:kt * 130 + 130], 1.0)
            ins = v.memset(vec2[:, 30:31], EPS_LN)
            dve.inc(ins)
        DVE(_)
        m_prelude = dve.n

        def _(s):
            s.wait_ge(sem_xb.h, 32)
            for c in range(8):
                s.dma_start(out=xT[:, c * T_TOT:(c + 1) * T_TOT],
                            in_=xbf_d[:, c * 128:(c + 1) * 128],
                            transpose=True).then_inc(sem_xT.h, 16)
        SYN(_)
        sem_xT.n += 8 * 16

        m_wq = [0] * 8
        m_wq_pe = [0] * 8

        def emit_wq_load(mm):
            def _(g):
                if mm >= 2:
                    pe.wait(g, m_wq_pe[mm - 2])
                for k in range(8):
                    g.dma_start(out=wqm[mm % 2][:, k * 128:(k + 1) * 128],
                                in_=w_qkv[k * 128:(k + 1) * 128,
                                          mm * 128:(mm + 1) * 128]
                                ).then_inc(sem_wq[mm % 2].h, 16)
            GPS(_)
            sem_wq[mm % 2].n += 128

        emit_wq_load(0)
        emit_wq_load(1)

        m_qt = [0] * 10
        m_qev = {}
        sqb_free = [(pe, 0), (pe, 0)]

        wq_queue = []
        m_wst_load = {}
        m_wst_dve = {}
        m_tern_dve = {}
        m_tern_st = {}

        def w_tile_src(t):
            if t < 32:
                return w1[t * 128:(t + 1) * 128, :]
            tt = t - 32
            return w2[(tt % 8) * 128:(tt % 8) * 128 + 128,
                      (tt // 8) * 1024:(tt // 8) * 1024 + 1024]

        def w_tile_dst(t):
            if t < 32:
                return w1t_d[t * 128:(t + 1) * 128, :]
            tt = t - 32
            return w2t_d[(tt % 8) * 128:(tt % 8) * 128 + 128,
                         (tt // 8) * 1024:(tt // 8) * 1024 + 1024]

        def mk_stats_load(t):
            def emit():
                def _(s):
                    if t >= 1:
                        dve.wait(s, m_wst_dve[t - 1])
                    s.dma_start(out=wstage[:, :],
                                in_=w_tile_src(t)).then_inc(sem_wl[t % 2].h, 16)
                SYN(_)
                sem_wl[t % 2].n += 16
            return emit

        def mk_stats_red(t):
            def emit():
                def _(v):
                    sem_wl[t % 2].wait_all(v)
                    col = 0 if t < 32 else 1
                    ins = v.reduce_sum(acc_w[:, 2:3], wstage[:, :],
                                       axis=AX.X, apply_absolute_value=True)
                    vsync(v, ins)
                    ins = v.tensor_add(acc_w[:, col:col + 1], acc_w[:, col:col + 1],
                                       acc_w[:, 2:3])
                    dve.inc(ins)
                DVE(_)
                m_wst_dve[t] = dve.n
            return emit

        m_thr = [0]

        def mk_s_finalize():
            def emit():
                def _(t):
                    dve.wait(t, m_wst_dve[63])
                    psx_acquire(t)
                    t.matmul(ps_x[0:1, 0:1], acc_w[:, 0:1], ones_fc[:, 0:1],
                             start=True, stop=True)
                    ins = t.matmul(ps_x[0:1, 1:2], acc_w[:, 1:2], ones_fc[:, 0:1],
                                   start=True, stop=True)
                    pe.inc(ins)
                PE(_)
                m1 = pe.n

                def _(v):
                    pe.wait(v, m1)
                    v.tensor_scalar(sml[0:1, 37:38], ps_x[0:1, 0:1],
                                    1.0 / (HID * D), None, ALU.mult)
                    ins = v.tensor_scalar(sml[0:1, 38:39], ps_x[0:1, 1:2],
                                          1.0 / (HID * D), None, ALU.mult)
                    vsync(v, ins)
                    v.tensor_scalar_max(sml[0:1, 37:38], sml[0:1, 37:38], EPS_Q)
                    ins = v.tensor_scalar_max(sml[0:1, 38:39], sml[0:1, 38:39],
                                              EPS_Q)
                    vsync(v, ins)
                    v.tensor_scalar_mul(sml[0:1, 39:40], sml[0:1, 37:38], 0.5)
                    ins = v.tensor_scalar_mul(sml[0:1, 40:41], sml[0:1, 38:39], 0.5)
                    dve.inc(ins)
                DVE(_)
                m2 = dve.n
                psx_free[0] = (dve, m2)

                def _(t):
                    dve.wait(t, m2)
                    psx_acquire(t)
                    t.matmul(ps_x[:, 0:1], ones_f[0:1, :], sml[0:1, 39:40],
                             start=True, stop=True)
                    t.matmul(ps_x[:, 1:2], ones_f[0:1, :], sml[0:1, 40:41],
                             start=True, stop=True)
                    t.matmul(ps_x[:, 2:3], ones_f[0:1, :], sml[0:1, 37:38],
                             start=True, stop=True)
                    ins = t.matmul(ps_x[:, 3:4], ones_f[0:1, :], sml[0:1, 38:39],
                                   start=True, stop=True)
                    pe.inc(ins)
                PE(_)
                m3 = pe.n

                def _(v):
                    pe.wait(v, m3)
                    v.tensor_copy(vec[:, 2:3], ps_x[:, 0:1])
                    v.tensor_scalar_mul(vec[:, 3:4], ps_x[:, 0:1], -1.0)
                    v.tensor_copy(vec[:, 4:5], ps_x[:, 1:2])
                    v.tensor_scalar_mul(vec[:, 5:6], ps_x[:, 1:2], -1.0)
                    v.tensor_copy(vec[:, 6:7], ps_x[:, 2:3])
                    ins = v.tensor_copy(vec[:, 7:8], ps_x[:, 3:4])
                    dve.inc(ins)
                DVE(_)
                psx_free[0] = (dve, dve.n)
                m_thr[0] = dve.n
            return emit

        def mk_tern_load(t):
            def emit():
                def _(s):
                    if t >= 1:
                        dve.wait(s, m_tern_dve[t - 1])
                    else:
                        dve.wait(s, m_wst_dve[63])
                    s.dma_start(out=wstage[:, :],
                                in_=w_tile_src(t)).then_inc(sem_wl[t % 2].h, 16)
                SYN(_)
                sem_wl[t % 2].n += 16
            return emit

        def mk_tern_comp(t):
            def emit():
                def _(v):
                    sem_wl[t % 2].wait_all(v)
                    if t >= 1:
                        sem_ts[(t - 1) % 2].wait_all(v)
                    w = wstage[:, :]
                    thr = vec[:, 2:3] if t < 32 else vec[:, 4:5]
                    nthr = vec[:, 3:4] if t < 32 else vec[:, 5:6]
                    v.tensor_scalar(tern_a[:, :], w, thr, None, ALU.is_ge)
                    ins = v.tensor_scalar(tern_b[:, :], w, nthr, None, ALU.is_le)
                    vsync(v, ins)
                    ins = v.tensor_sub(tern_c[:, 0:1024],
                                       tern_a[:, :], tern_b[:, :])
                    dve.inc(ins)
                DVE(_)
                m_tern_dve[t] = dve.n

                def _(s):
                    dve.wait(s, m_tern_dve[t])
                    s.dma_start(out=w_tile_dst(t),
                                in_=tern_c[:, 0:1024]
                                ).then_inc(sem_ts[t % 2].h, 16)
                SYN(_)
                sem_ts[t % 2].n += 16
            return emit

        for t in range(64):
            wq_queue.append(mk_stats_load(t))
            wq_queue.append(mk_stats_red(t))
        wq_queue.append(mk_s_finalize())
        for t in range(64):
            wq_queue.append(mk_tern_load(t))
            wq_queue.append(mk_tern_comp(t))

        m_kv = [0] * 10
        m_negM = [0] * 10
        m_pair_pe_done = [0] * 10
        m_pair_act_done = [0] * 10
        m_kvproj_done = [0] * 10

        def emit_wkv(p):
            pb = p % 2
            def _(g):
                if p >= 2:
                    pe.wait(g, m_kvproj_done[p - 2])
                for k in range(8):
                    g.dma_start(
                        out=wkv[pb][:, k * 256:k * 256 + 128],
                        in_=w_qkv[k * 128:(k + 1) * 128,
                                  D + p * 128:D + p * 128 + 128]
                    ).then_inc(sem_kv[pb].h, 16)
                    g.dma_start(
                        out=wkv[pb][:, k * 256 + 128:k * 256 + 256],
                        in_=w_qkv[k * 128:(k + 1) * 128,
                                  2 * D + p * 128:2 * D + p * 128 + 128]
                    ).then_inc(sem_kv[pb].h, 16)
            GPS(_)
            sem_kv[pb].n += 256

        def make_prep(p, A0ref):
            ops = []
            pb = p % 2

            def mk_qproj(n):
                def emit():
                    def _(t):
                        if n == 0:
                            sem_wq[p % 2].wait_all(t)
                            if p == 0:
                                sem_xT.wait_all(t)
                                gp.wait(t, m_ident)
                        psx_acquire(t)
                        ins = None
                        for k in range(8):
                            ins = t.matmul(
                                ps_x[:, :],
                                wqm[p % 2][:, k * 128:(k + 1) * 128],
                                xT[:, k * T_TOT + n * 512:
                                   k * T_TOT + n * 512 + 512],
                                start=(k == 0), stop=(k == 7))
                        pe.inc(ins)
                    PE(_)
                    m_mm = pe.n
                    m_wq_pe[p] = pe.n

                    def _(v):
                        pe.wait(v, m_mm)
                        ins = v.tensor_copy(
                            QT[:, p * T_OWN + n * 512:p * T_OWN + n * 512 + 512],
                            ps_x[:, :])
                        dve.inc(ins)
                    DVE(_)
                    psx_free[0] = (dve, dve.n)
                    m_qev[(p, n)] = dve.n
                    if n == 3:
                        m_qt[p] = dve.n
                return emit
            for n in range(4):
                ops.append(mk_qproj(n))

            def em_wqload():
                if p + 2 < NPAIR:
                    emit_wq_load(p + 2)
            ops.append(em_wqload)

            def mk_qnorm(j):
                def emit():
                    def _(g):
                        dve.wait(g, m_qev[(p, j)])
                        s_, v_ = sqb_free[j % 2]
                        s_.wait(g, v_)
                        sl = QT[:, p * T_OWN + j * 512:p * T_OWN + j * 512 + 512]
                        ins = g.tensor_tensor(sqb[j % 2][:, :], sl, sl, ALU.mult)
                        gp.inc(ins)
                    GPS(_)
                    m_sq = gp.n

                    def _(t):
                        gp.wait(t, m_sq)
                        psx_acquire(t)
                        t.matmul(ps_x[0:1, :], ones_bf[0:64, 0:1],
                                 sqb[j % 2][0:64, :], start=True, stop=True,
                                 tile_position=(0, 0))
                        ins = t.matmul(ps_x[32:33, :], ones_bf[64:128, 0:1],
                                       sqb[j % 2][64:128, :],
                                       start=True, stop=True,
                                       tile_position=(64, 32))
                        pe.inc(ins)
                    PE(_)
                    m_nm = pe.n
                    sqb_free[j % 2] = (pe, pe.n)

                    def _(v):
                        pe.wait(v, m_nm)
                        v.reduce_max(sml[0:1, 48 + j:49 + j], ps_x[0:1, :],
                                     axis=AX.X)
                        ins = v.reduce_max(sml[0:1, 52 + j:53 + j],
                                           ps_x[32:33, :], axis=AX.X)
                        dve.inc(ins)
                    DVE(_)
                    psx_free[0] = (dve, dve.n)
                return emit
            for j in range(4):
                ops.append(mk_qnorm(j))

            def em_qfin():
                def _(v):
                    dve.wait(v, dve.n)
                    v.reduce_max(sml[0:1, 2 * p:2 * p + 1], sml[0:1, 48:52],
                                 axis=AX.X)
                    ins = v.reduce_max(sml[0:1, 2 * p + 1:2 * p + 2],
                                       sml[0:1, 52:56], axis=AX.X)
                    dve.inc(ins)
                DVE(_)
            ops.append(em_qfin)

            def em_wkv():
                def _(g):
                    if p >= 2:
                        pe.wait(g, m_pair_pe_done[p - 2])
                    for k in range(8):
                        g.dma_start(
                            out=wkv[pb][:, k * 256:k * 256 + 128],
                            in_=w_qkv[k * 128:(k + 1) * 128,
                                      D + p * 128:D + p * 128 + 128]
                        ).then_inc(sem_kv[pb].h, 16)
                        g.dma_start(
                            out=wkv[pb][:, k * 256 + 128:k * 256 + 256],
                            in_=w_qkv[k * 128:(k + 1) * 128,
                                      2 * D + p * 128:2 * D + p * 128 + 128]
                        ).then_inc(sem_kv[pb].h, 16)
                GPS(_)
                sem_kv[pb].n += 256

            def mk_kchunk(n):
                def emit():
                    def _(t):
                        sem_kv[pb].wait_all(t)
                        if p < 2 and n == 0:
                            sem_xT.wait_all(t)
                        psx_acquire(t)
                        ins = None
                        for k in range(8):
                            ins = t.matmul(
                                ps_x[:, :],
                                wkv[pb][:, k * 256:k * 256 + 128],
                                xT[:, k * T_TOT + n * 512:k * T_TOT + n * 512 + 512],
                                start=(k == 0), stop=(k == 7))
                        pe.inc(ins)
                    PE(_)
                    m_mm = pe.n

                    def _(v):
                        pe.wait(v, m_mm)
                        if p >= 2 and n == 0:
                            pe.wait(v, m_pair_pe_done[p - 2])
                        ins = v.tensor_copy(
                            k_pair[pb][:, n * 512:(n + 1) * 512], ps_x[:, :])
                        dve.inc(ins)
                    DVE(_)
                    psx_free[0] = (dve, dve.n)
                    mk_done[n] = dve.n
                return emit
            mk_done = {}

            def mk_vchunk(n):
                st = {}

                def emit1():
                    def _(t):
                        psx_acquire(t)
                        ins = None
                        for k in range(8):
                            ins = t.matmul(
                                ps_x[:, :],
                                wkv[pb][:, k * 256 + 128:k * 256 + 256],
                                xT[:, k * T_TOT + n * 512:k * T_TOT + n * 512 + 512],
                                start=(k == 0), stop=(k == 7))
                        pe.inc(ins)
                    PE(_)
                    m_mm = pe.n

                    def _(v):
                        pe.wait(v, m_mm)
                        ins = v.tensor_copy(vfm[n % 2][:, :], ps_x[:, :])
                        dve.inc(ins)
                    DVE(_)
                    st["ev"] = dve.n
                    psx_free[0] = (dve, dve.n)

                def emit2():
                    def _(t):
                        dve.wait(t, st["ev"])
                        sm, vv = pst_free[0]
                        sm.wait(t, vv)
                        ins = None
                        for j in range(4):
                            ins = t.transpose(ps_t[:, j * 128:(j + 1) * 128],
                                              vfm[n % 2][:, j * 128:(j + 1) * 128],
                                              ident[:, :])
                        pe.inc(ins)
                    PE(_)
                    m_tr = pe.n

                    def _(v):
                        pe.wait(v, m_tr)
                        if p >= 2 and n == 0:
                            pe.wait(v, m_pair_pe_done[p - 2])
                        ins = None
                        for j in range(4):
                            kt = n * 4 + j
                            v.tensor_copy(
                                v_aug[pb][:, kt * 130:kt * 130 + 64],
                                ps_t[:, j * 128:j * 128 + 64])
                            ins = v.tensor_copy(
                                v_aug[pb][:, kt * 130 + 65:kt * 130 + 129],
                                ps_t[:, j * 128 + 64:j * 128 + 128])
                        dve.inc(ins)
                    DVE(_)
                    pst_free[0] = (dve, dve.n)
                    if n == 7:
                        m_kv[p] = dve.n
                        m_kvproj_done[p] = pe.n
                return [emit1, emit2]
            for n in range(8):
                ops.append(mk_kchunk(n))
                for c in mk_vchunk(n):
                    ops.append(c)

            mk_nm = {}

            def mk_knorm(n):
                def emit():
                    def _(g):
                        dve.wait(g, mk_done[n])
                        s_, v_ = sqb_free[n % 2]
                        s_.wait(g, v_)
                        sl = k_pair[pb][:, n * 512:(n + 1) * 512]
                        ins = g.tensor_tensor(sqb[n % 2][:, :], sl, sl, ALU.mult)
                        gp.inc(ins)
                    GPS(_)
                    m_sq = gp.n

                    def _(t):
                        gp.wait(t, m_sq)
                        psx_acquire(t)
                        t.matmul(ps_x[0:1, :], ones_bf[0:64, 0:1],
                                 sqb[n % 2][0:64, :], start=True, stop=True,
                                 tile_position=(0, 0))
                        ins = t.matmul(ps_x[32:33, :], ones_bf[64:128, 0:1],
                                       sqb[n % 2][64:128, :], start=True, stop=True,
                                       tile_position=(64, 32))
                        pe.inc(ins)
                    PE(_)
                    m_nm = pe.n
                    sqb_free[n % 2] = (pe, pe.n)

                    def _(v):
                        pe.wait(v, m_nm)
                        v.reduce_max(sml[0:1, 16 + n:17 + n], ps_x[0:1, :],
                                     axis=AX.X)
                        ins = v.reduce_max(sml[0:1, 24 + n:25 + n], ps_x[32:33, :],
                                           axis=AX.X)
                        dve.inc(ins)
                    DVE(_)
                    psx_free[0] = (dve, dve.n)
                return emit
            for n in range(8):
                ops.append(mk_knorm(n))

            def em_negM():
                def _(v):
                    dve.wait(v, dve.n)
                    v.reduce_max(sml[0:1, 32:33], sml[0:1, 16:24], axis=AX.X)
                    ins = v.reduce_max(sml[0:1, 33:34], sml[0:1, 24:32], axis=AX.X)
                    vsync(v, ins)
                    v.tensor_tensor(sml[0:1, 34:35], sml[0:1, 32:33],
                                    sml[0:1, 2 * p:2 * p + 1], ALU.add)
                    ins = v.tensor_tensor(sml[0:1, 35:36], sml[0:1, 33:34],
                                          sml[0:1, 2 * p + 1:2 * p + 2], ALU.add)
                    vsync(v, ins)
                    ins = v.tensor_tensor(sml[0:1, 36:37], sml[0:1, 34:35],
                                          sml[0:1, 35:36], ALU.max)
                    dve.inc(ins)
                DVE(_)
                m_t = dve.n

                def _(t):
                    dve.wait(t, m_t)
                    psx_acquire(t)
                    ins = t.matmul(ps_x[:, 0:1], negones_f[0:1, :],
                                   sml[0:1, 36:37], start=True, stop=True)
                    pe.inc(ins)
                PE(_)
                m_b = pe.n

                def _(v):
                    pe.wait(v, m_b)
                    if p >= 2:
                        act.wait_planned(v, m_pair_act_done[p - 2])
                    ins = v.tensor_scalar(vec[:, pb:pb + 1], ps_x[:, 0:1],
                                          1.0 / 16.0, None, ALU.mult)
                    dve.inc(ins)
                DVE(_)
                psx_free[0] = (dve, dve.n)
                m_negM[p] = dve.n
            ops.append(em_negM)
            return ops

        emit_wkv(0)
        emit_wkv(1)
        for emitfn in make_prep(0, 0):
            emitfn()
        prep_queue = list(make_prep(1, 0))

        A0 = act.n
        m_no = {}
        m_norm_done = {}
        raw_free = [(dve, 0), (dve, 0)]
        m_bc_prev = [0]
        norm_queue = []
        norms_emitted = [0]
        exp_qblks = []

        for p in range(NPAIR):
            pb = p % 2
            if p + 2 < NPAIR:
                emit_wkv(p + 2)
            for qb in range(NQB):
                gbase = (p * NQB + qb) * NKC
                for kc0 in range(0, NKC, 2):
                    def _(t, p=p, qb=qb, kc0=kc0, pb=pb):
                        for kc in (kc0, kc0 + 1):
                            G = (p * NQB + qb) * NKC + kc
                            sb_ = G % 2
                            if kc == 0:
                                dve.wait(t, m_kv[p])
                                if qb == 0:
                                    dve.wait(t, m_qt[p])
                            act.wait_planned(t, A0 + G - 1)
                            t.matmul(ps_s[sb_][:, 0:512],
                                     k_pair[pb][0:64, kc * 128:(kc + 1) * 128],
                                     QT[0:64, p * T_OWN + qb * 512:
                                        p * T_OWN + qb * 512 + 512],
                                     start=True, stop=True, tile_position=(0, 0))
                            ins = t.matmul(ps_s[sb_][:, 512:1024],
                                           k_pair[pb][64:128,
                                                      kc * 128:(kc + 1) * 128],
                                           QT[64:128, p * T_OWN + qb * 512:
                                              p * T_OWN + qb * 512 + 512],
                                           start=True, stop=True,
                                           tile_position=(64, 0))
                            pe.inc(ins)
                            m_sc[G] = pe.n
                            if kc >= 1:
                                Gp = G - 1
                                kcp = Gp % NKC
                                spb = Gp % 2
                                if kcp == 0:
                                    if qb > 0:
                                        dve.wait(t, m_no[(p, qb - 1)])
                                    elif p >= 1:
                                        dve.wait(t, m_no[(p - 1, NQB - 1)])
                                act.wait_planned(t, A0 + Gp + 1)
                                t.matmul(ps_oA[:, :],
                                         v_aug[pb][:, kcp * 130:kcp * 130 + 128],
                                         probs[spb][:, 0:512],
                                         start=(kcp == 0), stop=False)
                                ins = t.matmul(ps_oB[:, :],
                                               v_aug[pb][:, kcp * 130 + 65:
                                                          kcp * 130 + 193],
                                               probs[spb][:, 512:1024],
                                               start=(kcp == 0), stop=False)
                                pe.inc(ins)
                                m_pv[Gp] = pe.n
                    PE(_)
                    budget = 2
                    while budget > 0 and prep_queue:
                        prep_queue.pop(0)()
                        budget -= 1
                    if len(norm_queue) >= 2:
                        norm_queue.pop(0)()
                    elif budget > 0 and wq_queue:
                        wq_queue.pop(0)()

                exp_qblks.append((p, qb, pb))
                if qb == NQB - 1:
                    def _(sc, batch=list(exp_qblks)):
                        first = True
                        for (pp, qq, ppb) in batch:
                            for kc in range(NKC):
                                G = (pp * NQB + qq) * NKC + kc
                                sb_ = G % 2
                                if first:
                                    dve.wait(sc, m_negM[pp])
                                    first = False
                                pe.wait(sc, m_sc[G])
                                if G >= 2 and (G - 2) in m_pv:
                                    pe.wait(sc, m_pv[G - 2])
                                ins = sc.activation(probs[sb_][:, :],
                                                    ps_s[sb_][:, :], AF.Exp,
                                                    bias=vec[:, ppb:ppb + 1],
                                                    scale=0.125)
                                act.inc(ins)
                    ACTE(_)
                    exp_qblks.clear()
                    assert act.n == A0 + (p * NQB + NQB) * NKC, (act.n, A0)

                def _(t, p=p, qb=qb, pb=pb):
                    G = (p * NQB + qb) * NKC + (NKC - 1)
                    kcp = NKC - 1
                    act.wait_planned(t, A0 + G + 1)
                    t.matmul(ps_oA[:, :], v_aug[pb][:, kcp * 130:kcp * 130 + 128],
                             probs[G % 2][:, 0:512], start=False, stop=True)
                    ins = t.matmul(ps_oB[:, :],
                                   v_aug[pb][:, kcp * 130 + 65:kcp * 130 + 193],
                                   probs[G % 2][:, 512:1024], start=False,
                                   stop=True)
                    pe.inc(ins)
                    m_pv[G] = pe.n
                PE(_)
                if qb == NQB - 1:
                    m_pair_pe_done[p] = pe.n
                    m_pair_act_done[p] = A0 + (p + 1) * NQB * NKC

                rb = (2 * (p * NQB + qb)) % 4
                gq = p * NQB + qb
                assert gq < 2 or norms_emitted[0] >= gq - 1, \
                    (gq, norms_emitted[0])
                def _(v, p=p, qb=qb, rb=rb):
                    pe.wait(v, m_pv[(p * NQB + qb) * NKC + NKC - 1])
                    s_, v_ = raw_free[rb // 2]
                    s_.wait(v, v_)
                    v.tensor_copy(raw[rb][:, :], ps_oA[:, :])
                    ins = v.tensor_copy(raw[rb + 1][:, :], ps_oB[:, :])
                    dve.inc(ins)
                DVE(_)
                m_no[(p, qb)] = dve.n

                def mk_norm(p=p, qb=qb, rb=rb):
                    def emit():
                        def _(v):
                            dve.wait(v, m_no[(p, qb)])
                            pe.wait(v, m_bc_prev[0])
                            v.tensor_copy(den2[0:1, 0:512], raw[rb][64:65, :])
                            ins = v.tensor_copy(den2[0:1, 512:1024],
                                                raw[rb + 1][64:65, :])
                            dve.inc(ins)
                        DVE(_)
                        m_dc = dve.n

                        def _(t):
                            dve.wait(t, m_dc)
                            psx_acquire(t)
                            t.matmul(ps_x[0:64, :], onesrow_bf[0:1, 0:64],
                                     den2[0:1, 0:512], start=True, stop=True,
                                     tile_position=(0, 0))
                            ins = t.matmul(ps_x[64:128, :], onesrow_bf[0:1, 0:64],
                                           den2[0:1, 512:1024], start=True,
                                           stop=True, tile_position=(0, 64))
                            pe.inc(ins)
                        PE(_)
                        m_bc = pe.n
                        m_bc_prev[0] = m_bc

                        def _(v):
                            pe.wait(v, m_bc)
                            v.reciprocal(rec[0:64, 0:512], ps_x[0:64, :])
                            ins = v.reciprocal(rec[0:64, 512:1024],
                                               ps_x[64:128, :])
                            vsync(v, ins)
                            col = p * T_OWN + qb * 512
                            v.tensor_tensor(headoutT[0:64, col:col + 512],
                                            raw[rb][0:64, :], rec[0:64, 0:512],
                                            ALU.mult)
                            ins = v.tensor_tensor(headoutT[64:128, col:col + 512],
                                                  raw[rb + 1][0:64, :],
                                                  rec[0:64, 512:1024], ALU.mult)
                            dve.inc(ins)
                        DVE(_)
                        psx_free[0] = (dve, dve.n)
                        raw_free[rb // 2] = (dve, dve.n)
                        m_norm_done[(p, qb)] = dve.n
                        norms_emitted[0] += 1
                    return emit
                norm_queue.append(mk_norm())

            while prep_queue:
                prep_queue.pop(0)()
            if p + 2 < NPAIR:
                prep_queue = list(make_prep(p + 2, A0))

        while norm_queue:
            norm_queue.pop(0)()
        while wq_queue:
            wq_queue.pop(0)()

        m_attn_pe = pe.n
        m_attn_dve = dve.n
        m_attn_act = act.n
        m_headout = max(m_norm_done.values())

        if debug:
            def _(s):
                dve.wait(s, m_headout)
                s.dma_start(out=ho_d[:, :], in_=headoutT[:, :]).then_inc(sem_s.h, 16)
            SYN(_)
            sem_s.n += 16

        s_att.close()

        w1T = SB(s_mid, "w1T", [128, 8 * 4096], BF16)
        w2T = SB(s_mid, "w2T", [128, 32 * 1024], BF16)
        wout_sb = SB(s_2a, "wout_sb", [128, 8 * 1024], BF16)
        x_res = SB(s_2a, "x_res", [128, 3 * 1024], F32)
        r1b = SB(s_2a, "r1b", [128, 2 * 1024], F32)
        x1st = SB(s_2a, "x1st", [128, 1024], F32)
        ps_a = [PS(s_2a, f"ps_a{i}", [128, 1024], F32) for i in range(2)]

        def _(g):
            pe.wait(g, m_attn_pe)
            dve.wait(g, m_attn_dve)
            act.wait_planned(g, m_attn_act)
            sem_ts[0].wait_all(g)
            sem_ts[1].wait_all(g)
            for c in range(8):
                g.dma_start(out=wout_sb[:, c * 1024:(c + 1) * 1024],
                            in_=w_out[c * 128:(c + 1) * 128, :]).then_inc(sem_g.h, 16)
        GPS(_)
        sem_g.n += 128
        m_wout = sem_g.n

        def _(s):
            pe.wait(s, m_attn_pe)
            dve.wait(s, m_attn_dve)
            sem_ts[0].wait_all(s)
            sem_ts[1].wait_all(s)
            for c in range(8):
                s.dma_start(out=w1T[:, c * 4096:(c + 1) * 4096],
                            in_=w1t_d[:, c * 128:(c + 1) * 128],
                            transpose=True).then_inc(sem_wT.h, 16)
        SYN(_)
        sem_wT.n += 128
        m_w1T = sem_wT.n
        def _(s):
            sem_ts[0].wait_all(s)
            sem_ts[1].wait_all(s)
            for c in range(32):
                s.dma_start(out=w2T[:, c * 1024:(c + 1) * 1024],
                            in_=w2t_d[:, c * 128:(c + 1) * 128],
                            transpose=True).then_inc(sem_wT.h, 16)
        SYN(_)
        sem_wT.n += 32 * 16
        m_w2T = sem_wT.n

        m_r1_dve = [0] * NSLAB
        m_ln1_act = [0] * NSLAB
        m_rstd1 = [0] * NSLAB
        m_sqa1 = {}

        def emit_ln1_norm(sc, s_i):
            dve.wait(sc, m_rstd1[s_i])
            if s_i >= 1:
                sem_x1s[(s_i - 1) % 2].wait_all(sc)
            b = 8 * (s_i % 2)
            r1 = r1b[:, (s_i % 2) * 1024:(s_i % 2) * 1024 + 1024]
            ins = sc.activation(
                x1st[:, :], r1,
                AF.Identity, scale=vec2[:, b + 4:b + 5],
                bias=vec2[:, b + 5:b + 6])
            act.inc(ins)
            m_ln1_act[s_i] = act.n

        def emit_x1_store(s_i):
            def _(s, s_i=s_i):
                act.wait_planned(s, m_ln1_act[s_i])
                s.dma_start(out=x1_d[s_i * 128:(s_i + 1) * 128, :],
                            in_=x1st[:, :]).then_inc(sem_x1s[s_i % 2].h, 16)
            SYN(_)
            sem_x1s[s_i % 2].n += 16

        for s_i in range(NSLAB):
            def _(s, s_i=s_i):
                if s_i == 0:
                    pe.wait(s, m_attn_pe)
                    dve.wait(s, m_attn_dve)
                if s_i >= 3:
                    dve.wait(s, m_r1_dve[s_i - 3])
                s.dma_start(out=x_res[:, (s_i % 3) * 1024:(s_i % 3) * 1024 + 1024],
                            in_=x_own[s_i * 128:(s_i + 1) * 128, :]
                            ).then_inc(sem_xr[s_i % 3].h, 16)
            SYN(_)
            sem_xr[s_i % 3].n += 16

            def _(t, s_i=s_i):
                if s_i == 0:
                    sem_g.wait(t, m_wout)
                    sem_xb.wait(t, 48)
                    dve.wait(t, m_headout)
                if s_i >= 2:
                    dve.wait(t, m_r1_dve[s_i - 2])
                pa = ps_a[s_i % 2]
                for half in range(2):
                    for c in range(8):
                        t.matmul(
                            pa[:, half * 512:half * 512 + 512],
                            headoutT[:, c * T_OWN + s_i * 128:
                                     c * T_OWN + s_i * 128 + 128],
                            wout_sb[:, c * 1024 + half * 512:
                                    c * 1024 + half * 512 + 512],
                            start=(c == 0), stop=False)
                    ins = t.matmul(pa[:, half * 512:half * 512 + 512],
                                   onesrow_bf[0:1, :],
                                   b_out_bf[0:1, half * 512:half * 512 + 512],
                                   start=False, stop=True)
                    pe.inc(ins)
            PE(_)
            m_op = pe.n

            def _(v, s_i=s_i, m_op=m_op):
                pe.wait(v, m_op)
                sem_xr[s_i % 3].wait_all(v)
                if s_i >= 2:
                    act.wait_planned(v, m_ln1_act[s_i - 2])
                if s_i >= 1 and (s_i - 1) in m_sqa1:
                    act.wait_planned(v, m_sqa1[s_i - 1])
                b = 8 * (s_i % 2)
                r1 = r1b[:, (s_i % 2) * 1024:(s_i % 2) * 1024 + 1024]
                ins = v.scalar_tensor_tensor(
                    r1, ps_a[s_i % 2][:, :], 1.0,
                    x_res[:, (s_i % 3) * 1024:(s_i % 3) * 1024 + 1024],
                    ALU.mult, ALU.add, accum_out=vec2[:, b:b + 1])
                vsync(v, ins)
                ins = v.tensor_scalar_mul(vec2[:, b + 1:b + 2], vec2[:, b:b + 1],
                                          -1.0 / D)
                dve.inc(ins)
            DVE(_)
            m_r1 = dve.n
            m_r1_dve[s_i] = m_r1

            def _(sc, s_i=s_i, m_r1=m_r1):
                if s_i >= 1:
                    emit_ln1_norm(sc, s_i - 1)
                dve.wait(sc, m_r1)
                b = 8 * (s_i % 2)
                r1 = r1b[:, (s_i % 2) * 1024:(s_i % 2) * 1024 + 1024]
                scr1 = r1b[:, ((s_i + 1) % 2) * 1024:((s_i + 1) % 2) * 1024 + 1024]
                ins = sc.activation(scr1, r1, AF.Square,
                                    bias=vec2[:, b + 1:b + 2],
                                    accum_out=vec2[:, b + 2:b + 3])
                asyncb(sc, ins)
                ins = sc.activation(vec2[:, b + 3:b + 4], vec2[:, b + 2:b + 3],
                                    AF.Sqrt, scale=1.0 / D, bias=vec2[:, 30:31])
                act.inc(ins)
            ACTE(_)
            m_sqa = act.n
            m_sqa1[s_i] = act.n
            if s_i >= 1:
                emit_x1_store(s_i - 1)

            def _(v, s_i=s_i, m_sqa=m_sqa):
                act.wait_planned(v, m_sqa)
                b = 8 * (s_i % 2)
                ins = v.reciprocal(vec2[:, b + 4:b + 5], vec2[:, b + 3:b + 4])
                vsync(v, ins)
                ins = v.tensor_mul(vec2[:, b + 5:b + 6], vec2[:, b + 1:b + 2],
                                   vec2[:, b + 4:b + 5])
                dve.inc(ins)
            DVE(_)
            m_rstd1[s_i] = dve.n

        def _(sc):
            emit_ln1_norm(sc, NSLAB - 1)
        ACTE(_)
        emit_x1_store(NSLAB - 1)

        m_2a_pe = pe.n
        m_2a_dve = dve.n
        m_2a_act = act.n

        s_att2a.close()
        s_2a.close()

        x1b = SB(s_2b, "x1b", [128, 3 * 1024], F32)
        h1 = SB(s_2b, "h1", [128, 4096], F32)
        h1q = SB(s_2b, "h1q", [128, 4096], BF16)
        x1q = SB(s_2b, "x1q", [128, 1024], BF16)
        x1qT = SB(s_2b, "x1qT", [128, 2 * 1024], BF16)
        x2qT = SB(s_2b, "x2qT", [128, 4096], BF16)
        i32a = SB(s_2b, "i32a", [128, 1024], I32)
        i32b = SB(s_2b, "i32b", [128, 2 * 1024], I32)
        r2b = SB(s_2b, "r2b", [128, 2 * 1024], F32)
        o_sb = SB(s_2b, "o_sb", [128, 1024], F32)
        ps_h = [PS(s_2b, f"ps_h{i}", [128, 1024], F32) for i in range(2)]
        ps_f = PS(s_2b, "ps_f", [128, 1024], F32)
        ps_t2 = PS(s_2b, "ps_t2", [128, 1024], BF16)


        pst2_free = [(dve, 0)]
        m_q1v = {}
        m_gpx = {}
        m_tr1 = {}
        m_x1qT = {}
        m_gelu = {}
        m_q2 = {}
        m_q2q = {}
        m_gph = {}
        m_trh = {}
        m_x2qT = {}
        m_mm2 = {}
        m_mm1d = {}
        m_r2 = {}
        m_ln2 = {}
        m_rstd2 = {}
        m_sq2a_d = {}

        def vb_of(j):
            return 16 + 6 * (j % 2)

        def emit_stage1(j):
            def _(s, j=j):
                if j == 0:
                    pe.wait(s, m_2a_pe)
                    dve.wait(s, m_2a_dve)
                    act.wait_planned(s, m_2a_act)
                    sem_x1s[0].wait_all(s)
                    sem_x1s[1].wait_all(s)
                if j >= 3:
                    dve.wait(s, m_r2[j - 3])
                s.dma_start(out=x1b[:, (j % 3) * 1024:(j % 3) * 1024 + 1024],
                            in_=x1_d[j * 128:(j + 1) * 128, :]
                            ).then_inc(sem_y[j % 3].h, 16)
            SYN(_)
            sem_y[j % 3].n += 16

            def _(v, j=j):
                sem_y[j % 3].wait_all(v)
                if j >= 1:
                    gp.wait(v, m_gpx[j - 1])
                if j >= 2:
                    act.wait_planned(v, m_gelu[(j - 2, 3)])
                vb = vb_of(j)
                x1 = x1b[:, (j % 3) * 1024:(j % 3) * 1024 + 1024]
                ins = v.reduce_max(vec2[:, 28:29], x1, axis=AX.X,
                                   apply_absolute_value=True)
                vsync(v, ins)
                ins = v.tensor_scalar_max(vec2[:, 28:29], vec2[:, 28:29], EPS_Q)
                vsync(v, ins)
                ins = v.reciprocal(vec2[:, 29:30], vec2[:, 28:29])
                vsync(v, ins)
                v.tensor_scalar_mul(vec2[:, vb:vb + 1], vec2[:, 29:30], 127.0)
                ins = v.tensor_scalar_mul(vec2[:, 28:29], vec2[:, 28:29],
                                          1.0 / 127.0)
                vsync(v, ins)
                ins = v.tensor_mul(vec2[:, vb + 1:vb + 2], vec2[:, 28:29],
                                   vec[:, 6:7])
                vsync(v, ins)
                ins = v.tensor_scalar(i32a[:, :], x1, vec2[:, vb:vb + 1], None,
                                      ALU.mult)
                dve.inc(ins)
            DVE(_)
            m_q1v[j] = dve.n

            def _(g, j=j):
                dve.wait(g, m_q1v[j])
                if j >= 1:
                    pe.wait(g, m_tr1[j - 1])
                ins = g.tensor_copy(x1q[:, :], i32a[:, :])
                gp.inc(ins)
            GPS(_)
            m_gpx[j] = gp.n


        def emit_stage1_pe(j):
            def _(t, j=j):
                gp.wait(t, m_gpx[j])
                sm, vv = pst2_free[0]
                sm.wait(t, vv)
                ins = None
                for jj in range(8):
                    ins = t.transpose(ps_t2[:, jj * 128:(jj + 1) * 128],
                                      x1q[:, jj * 128:(jj + 1) * 128], ident[:, :])
                pe.inc(ins)
            PE(_)
            m_tr1[j] = pe.n

            def _(v, j=j):
                pe.wait(v, m_tr1[j])
                ins = v.tensor_copy(x1qT[:, (j % 2) * 1024:(j % 2) * 1024 + 1024],
                                    ps_t2[:, :])
                dve.inc(ins)
            DVE(_)
            m_x1qT[j] = dve.n
            pst2_free[0] = (dve, dve.n)

        def emit_mm1(j, part):
            for q in (0, 1) if part == 0 else (2, 3):
                def _(t, j=j, q=q):
                    if q == 0:
                        dve.wait(t, m_x1qT[j])
                        if j == 0:
                            sem_wT.wait(t, m_w1T)
                    ga = m_gelu.get((j, q - 2), m_gelu.get((j - 1, q + 2), 0))
                    act.wait_planned(t, ga)
                    ph = ps_h[q % 2]
                    xoff = (j % 2) * 1024
                    for half in range(2):
                        ins = None
                        for k in range(8):
                            ins = t.matmul(
                                ph[:, half * 512:half * 512 + 512],
                                x1qT[:, xoff + k * 128:xoff + (k + 1) * 128],
                                w1T[:, k * 4096 + q * 1024 + half * 512:
                                    k * 4096 + q * 1024 + half * 512 + 512],
                                start=(k == 0), stop=(k == 7))
                        pe.inc(ins)
                PE(_)
                m_mm1d[(j, q)] = pe.n

        def emit_act_a(j, part):
            def _(sc, j=j, part=part):
                if part == 0 and j >= 2 and (j - 2) in m_rstd2:
                    jj = j - 2
                    dve.wait(sc, m_rstd2[jj])
                    if jj >= 1:
                        sem_op[(jj - 1) % 2].wait_all(sc)
                    b = 8 * (jj % 2)
                    r2 = r2b[:, (jj % 2) * 1024:(jj % 2) * 1024 + 1024]
                    ins = sc.activation(o_sb[:, :], r2, AF.Identity,
                                        scale=vec2[:, b + 4:b + 5],
                                        bias=vec2[:, b + 5:b + 6])
                    act.inc(ins)
                    m_ln2[jj] = act.n
                for q in (0, 1) if part == 0 else (2, 3):
                    pe.wait(sc, m_mm1d[(j, q)])
                    if q == 0:
                        dve.wait(sc, m_q1v[j])
                        if j >= 1:
                            dve.wait(sc, m_q2[j - 1])
                    vb = vb_of(j)
                    ins = sc.activation(h1[:, q * 1024:(q + 1) * 1024],
                                        ps_h[q % 2][:, :], AF.Gelu,
                                        scale=vec2[:, vb + 1:vb + 2])
                    act.inc(ins)
                    m_gelu[(j, q)] = act.n
            ACTE(_)

        def emit_store(j):
            def _(s, j=j):
                act.wait_planned(s, m_ln2[j])
                s.dma_start(out=out[j * 128:(j + 1) * 128, :],
                            in_=o_sb[:, :]).then_inc(sem_op[j % 2].h, 16)
            SYN(_)
            sem_op[j % 2].n += 16

        def emit_q2conv(j):
            def _(v, j=j):
                act.wait_planned(v, m_gelu[(j, 3)])
                vb = vb_of(j)
                ins = v.reduce_max(vec2[:, 28:29], h1[:, :], axis=AX.X,
                                   apply_absolute_value=True)
                vsync(v, ins)
                ins = v.tensor_scalar_max(vec2[:, 28:29], vec2[:, 28:29], EPS_Q)
                vsync(v, ins)
                ins = v.reciprocal(vec2[:, 29:30], vec2[:, 28:29])
                vsync(v, ins)
                v.tensor_scalar_mul(vec2[:, vb + 2:vb + 3], vec2[:, 29:30], 127.0)
                ins = v.tensor_scalar_mul(vec2[:, 28:29], vec2[:, 28:29],
                                          1.0 / 127.0)
                vsync(v, ins)
                ins = v.tensor_mul(vec2[:, vb + 3:vb + 4], vec2[:, 28:29],
                                   vec[:, 7:8])
                dve.inc(ins)
            DVE(_)

            for q in range(4):
                def _(v, j=j, q=q):
                    if q < 2:
                        cg = m_gph.get((j - 1, q + 2), 0)
                    else:
                        cg = m_gph[(j, q - 2)]
                    act.wait_planned(v, cg)
                    vb = vb_of(j)
                    ins = v.tensor_scalar(
                        i32b[:, (q % 2) * 1024:(q % 2) * 1024 + 1024],
                        h1[:, q * 1024:(q + 1) * 1024],
                        vec2[:, vb + 2:vb + 3], None, ALU.mult)
                    dve.inc(ins)
                DVE(_)
                m_q2q[(j, q)] = dve.n

                def _(sc, j=j, q=q):
                    dve.wait(sc, m_q2q[(j, q)])
                    tg = m_trh.get((j - 1, q), 0)
                    pe.wait(sc, tg)
                    ins = sc.activation(
                        h1q[:, q * 1024:(q + 1) * 1024],
                        i32b[:, (q % 2) * 1024:(q % 2) * 1024 + 1024], AF.Copy)
                    act.inc(ins)
                ACTE(_)
                m_gph[(j, q)] = act.n
            m_q2[j] = dve.n

        def emit_trh(j):
            for q in range(4):
                def _(t, j=j, q=q):
                    act.wait_planned(t, m_gph[(j, q)])
                    sm, vv = pst2_free[0]
                    sm.wait(t, vv)
                    ins = None
                    for jj in range(8):
                        ins = t.transpose(
                            ps_t2[:, jj * 128:(jj + 1) * 128],
                            h1q[:, q * 1024 + jj * 128:q * 1024 + (jj + 1) * 128],
                            ident[:, :])
                    pe.inc(ins)
                PE(_)
                m_trh[(j, q)] = pe.n

                def _(v, j=j, q=q):
                    pe.wait(v, m_trh[(j, q)])
                    ins = v.tensor_copy(x2qT[:, q * 1024:(q + 1) * 1024],
                                        ps_t2[:, :])
                    dve.inc(ins)
                DVE(_)
                pst2_free[0] = (dve, dve.n)
            m_x2qT[j] = dve.n

        def emit_mm2(j):
            def _(t, j=j):
                dve.wait(t, m_x2qT[j])
                if j == 0:
                    sem_wT.wait(t, m_w2T)
                if j >= 1:
                    dve.wait(t, m_r2[j - 1])
                for half in range(2):
                    ins = None
                    for kc in range(32):
                        ins = t.matmul(
                            ps_f[:, half * 512:half * 512 + 512],
                            x2qT[:, kc * 128:(kc + 1) * 128],
                            w2T[:, kc * 1024 + half * 512:
                                kc * 1024 + half * 512 + 512],
                            start=(kc == 0), stop=(kc == 31))
                    pe.inc(ins)
            PE(_)
            m_mm2[j] = pe.n

        def emit_tail_a(j):
            def _(v, j=j):
                pe.wait(v, m_mm2[j])
                if j >= 2:
                    act.wait_planned(v, m_ln2[j - 2])
                if j >= 1:
                    act.wait_planned(v, m_sq2a_d[j - 1])
                b = 8 * (j % 2)
                vb = vb_of(j)
                r2 = r2b[:, (j % 2) * 1024:(j % 2) * 1024 + 1024]
                ins = v.scalar_tensor_tensor(
                    r2, ps_f[:, :], vec2[:, vb + 3:vb + 4],
                    x1b[:, (j % 3) * 1024:(j % 3) * 1024 + 1024],
                    ALU.mult, ALU.add, accum_out=vec2[:, b:b + 1])
                vsync(v, ins)
                ins = v.tensor_scalar_mul(vec2[:, b + 1:b + 2], vec2[:, b:b + 1],
                                          -1.0 / D)
                dve.inc(ins)
            DVE(_)
            m_r2v = dve.n
            m_r2[j] = m_r2v

            def _(sc, j=j, m_r2v=m_r2v):
                dve.wait(sc, m_r2v)
                b = 8 * (j % 2)
                r2 = r2b[:, (j % 2) * 1024:(j % 2) * 1024 + 1024]
                if j == NSLAB - 1:
                    scr = h1[:, 0:1024]
                else:
                    scr = r2b[:, ((j + 1) % 2) * 1024:((j + 1) % 2) * 1024 + 1024]
                ins = sc.activation(scr, r2, AF.Square,
                                    bias=vec2[:, b + 1:b + 2],
                                    accum_out=vec2[:, b + 2:b + 3])
                asyncb(sc, ins)
                ins = sc.activation(vec2[:, b + 3:b + 4], vec2[:, b + 2:b + 3],
                                    AF.Sqrt, scale=1.0 / D, bias=vec2[:, 30:31])
                act.inc(ins)
            ACTE(_)
            m_sq2a = act.n
            m_sq2a_d[j] = act.n

            def _(v, j=j, m_sq2a=m_sq2a):
                act.wait_planned(v, m_sq2a)
                b = 8 * (j % 2)
                ins = v.reciprocal(vec2[:, b + 4:b + 5], vec2[:, b + 3:b + 4])
                vsync(v, ins)
                ins = v.tensor_mul(vec2[:, b + 5:b + 6], vec2[:, b + 1:b + 2],
                                   vec2[:, b + 4:b + 5])
                dve.inc(ins)
            DVE(_)
            m_rstd2[j] = dve.n

        emit_stage1(0)
        emit_stage1_pe(0)
        emit_stage1(1)
        emit_mm1(0, 0)
        emit_act_a(0, 0)
        emit_mm1(0, 1)
        emit_act_a(0, 1)
        emit_q2conv(0)
        for i in range(NSLAB):
            if i + 1 < NSLAB:
                emit_stage1_pe(i + 1)
            if i + 2 < NSLAB:
                emit_stage1(i + 2)
            if i + 1 < NSLAB:
                emit_mm1(i + 1, 0)
                emit_act_a(i + 1, 0)
                emit_mm1(i + 1, 1)
                emit_act_a(i + 1, 1)
                if i >= 1:
                    emit_store(i - 1)
            emit_trh(i)
            emit_mm2(i)
            if i + 1 < NSLAB:
                emit_q2conv(i + 1)
            emit_tail_a(i)

        for jj in (NSLAB - 2, NSLAB - 1):
            def _(sc, jj=jj):
                dve.wait(sc, m_rstd2[jj])
                sem_op[(jj - 1) % 2].wait_all(sc)
                b = 8 * (jj % 2)
                r2 = r2b[:, (jj % 2) * 1024:(jj % 2) * 1024 + 1024]
                ins = sc.activation(o_sb[:, :], r2, AF.Identity,
                                    scale=vec2[:, b + 4:b + 5],
                                    bias=vec2[:, b + 5:b + 6])
                act.inc(ins)
            ACTE(_)
            m_ln2[jj] = act.n
            emit_store(jj)

        if debug:
            def _(s):
                s.wait_ge(sem_op[0].h, 8 * 16)
                s.wait_ge(sem_op[1].h, 8 * 16)
                s.wait_ge(sem_s.h, 16)
                s.dma_start(out=vec_d[:, 0:32], in_=vec[:, :]).then_inc(sem_o.h, 16)
                s.dma_start(out=vec_d[:, 32:64], in_=vec2[:, :]).then_inc(sem_o.h, 16)
                s.dma_start(out=vec_d[0:1, 64:96],
                            in_=sml[0:1, 0:32]).then_inc(sem_o.h, 16)
                s.wait_ge(sem_o.h, 48)
            SYN(_)
            sem_o.n += 48
        def _(s):
            s.wait_ge(sem_op[0].h, 8 * 16)
            s.wait_ge(sem_op[1].h, 8 * 16)
            if debug:
                s.wait_ge(sem_o.h, 48)
        SYN(_)
        if debug:
            sem_s.maxw = max(sem_s.maxw, 16)
        sem_op[0].maxw = max(sem_op[0].maxw, 8 * 16)
        sem_op[1].maxw = max(sem_op[1].maxw, 8 * 16)

        s_2b.close()
        s_mid.close()

        allsems = [sem_g, sem_s, sem_o, pe, act, dve, gp, sem_xb, sem_xT,
                   sem_wT] + sem_wl + sem_ts + sem_xr + sem_x1s + sem_y + \
            sem_op + sem_wq + sem_kv
        for s in allsems:
            assert s.maxw <= s.n, f"{s.name}: waited {s.maxw} > total {s.n}"

    return nc


_NC = None


def kernel(L0_attn_x, L0_attn_w_qkv, L0_attn_w_out, L0_attn_b_out,
           L0_ffn_w1, L0_ffn_w2):
    global _NC
    x = np.ascontiguousarray(np.asarray(L0_attn_x, dtype=np.float32))
    wqkv = np.ascontiguousarray(np.asarray(L0_attn_w_qkv, dtype=np.float32))
    wout = np.ascontiguousarray(np.asarray(L0_attn_w_out, dtype=np.float32))
    bout = np.ascontiguousarray(np.asarray(L0_attn_b_out, dtype=np.float32))
    w1v = np.ascontiguousarray(np.asarray(L0_ffn_w1, dtype=np.float32))
    w2v = np.ascontiguousarray(np.asarray(L0_ffn_w2, dtype=np.float32))

    if _NC is None:
        _NC = build_program()

    in_maps = []
    for c in range(8):
        b, h = c // 2, c % 2
        xo = np.ascontiguousarray(x[b, h * T_OWN:(h + 1) * T_OWN])
        xt = np.ascontiguousarray(x[b, (1 - h) * T_OWN:(2 - h) * T_OWN])
        in_maps.append(dict(x_own=xo, x_oth=xt, w_qkv=wqkv, w_out=wout,
                            b_out=bout, w1=w1v, w2=w2v))
    res = run_bass_kernel_spmd(_NC, in_maps, core_ids=list(range(8)))
    y = np.empty((B_, SEQ, D), np.float32)
    for c in range(8):
        b, h = c // 2, c % 2
        y[b, h * T_OWN:(h + 1) * T_OWN] = res.results[c]["out"]
    return y


# version 21
# speedup vs baseline: 1.0074x; 1.0074x over previous
import contextlib
import numpy as np

import concourse.bass as bass
import concourse.mybir as mybir
from concourse.bass_utils import run_bass_kernel_spmd

F32 = mybir.dt.float32
BF16 = mybir.dt.bfloat16
I32 = mybir.dt.int32
AF = mybir.ActivationFunctionType
ALU = mybir.AluOpType
AX = mybir.AxisListType

B_, SEQ, D = 4, 4096, 1024
T_OWN = 2048
T_TOT = 4096
HID = 4096
NPAIR = 8
NQB = 4
QB = 512
NKC = 32
NSLAB = 16
EPS_LN = 1e-5
EPS_Q = 1e-5


class Sem:
    def __init__(self, nc, es, name):
        self.h = es.enter_context(nc.semaphore(name))
        self.name = name
        self.n = 0
        self.maxw = 0

    def inc(self, instr, k=1):
        instr.then_inc(self.h, k)
        self.n += k
        return self.n

    def wait(self, eng, val):
        assert val <= self.n, f"{self.name}: forward wait {val} > emitted {self.n}"
        if val <= 0:
            return
        self.maxw = max(self.maxw, val)
        eng.wait_ge(self.h, val)

    def wait_planned(self, eng, val):
        if val <= 0:
            return
        self.maxw = max(self.maxw, val)
        eng.wait_ge(self.h, val)

    def wait_all(self, eng):
        self.wait(eng, self.n)


def build_program(debug=False):
    nc = bass.Bass("TRN2", target_bir_lowering=False, debug=False)

    x_own = nc.dram_tensor("x_own", [T_OWN, D], F32, kind="ExternalInput").ap()
    x_oth = nc.dram_tensor("x_oth", [T_OWN, D], F32, kind="ExternalInput").ap()
    w_qkv = nc.dram_tensor("w_qkv", [D, 3 * D], F32, kind="ExternalInput").ap()
    w_out = nc.dram_tensor("w_out", [D, D], F32, kind="ExternalInput").ap()
    b_out = nc.dram_tensor("b_out", [D], F32, kind="ExternalInput").ap()
    w1 = nc.dram_tensor("w1", [HID, D], F32, kind="ExternalInput").ap()
    w2 = nc.dram_tensor("w2", [D, HID], F32, kind="ExternalInput").ap()
    out = nc.dram_tensor("out", [T_OWN, D], F32, kind="ExternalOutput").ap()

    xbf_d = nc.dram_tensor("xbf_d", [T_TOT, D], BF16, kind="Internal").ap()
    dbg_kind = "ExternalOutput" if debug else "Internal"
    w1t_d = nc.dram_tensor("w1t_d", [HID, D], BF16, kind=dbg_kind).ap()
    w2t_d = nc.dram_tensor("w2t_d", [D, HID], BF16, kind=dbg_kind).ap()
    ho_d = (nc.dram_tensor("ho_d", [128, 8 * T_OWN], BF16,
                           kind="ExternalOutput").ap() if debug else None)
    vec_d = (nc.dram_tensor("vec_d", [128, 96], F32,
                            kind="ExternalOutput").ap() if debug else None)
    x1_d = nc.dram_tensor("x1_d", [T_OWN, D], F32,
                          kind="ExternalOutput" if debug else "Internal").ap()

    m_sc = {}
    m_pv = {}

    es = contextlib.ExitStack()
    s_att2a = contextlib.ExitStack()
    s_att = contextlib.ExitStack()
    s_mid = contextlib.ExitStack()
    s_2a = contextlib.ExitStack()
    s_2b = contextlib.ExitStack()

    with es:
        def SB(stack, name, shape, dt):
            return stack.enter_context(nc.sbuf_tensor(name, shape, dt))

        def PS(stack, name, shape, dt):
            return stack.enter_context(nc.psum_tensor(name, shape, dt))

        ident = SB(es, "ident", [128, 128], BF16)
        ones_bf = SB(es, "ones_bf", [128, 2], BF16)
        ones_fc = SB(es, "ones_fc", [128, 1], F32)
        ones_f = SB(es, "ones_f", [1, 128], F32)
        negones_f = SB(es, "negones_f", [1, 128], F32)
        onesrow_bf = SB(es, "onesrow_bf", [1, 128], BF16)
        b_out_bf = SB(es, "b_out_bf", [1, D], BF16)
        vec = SB(es, "vec", [128, 32], F32)
        vec2 = SB(es, "vec2", [128, 32], F32)
        sml = SB(es, "sml", [1, 64], F32)
        acc_w = SB(es, "acc_w", [128, 4], F32)

        sem_g = Sem(nc, es, "sem_g")
        sem_s = Sem(nc, es, "sem_s")
        sem_o = Sem(nc, es, "sem_o")
        sem_xb = Sem(nc, es, "sem_xb")
        sem_xT = Sem(nc, es, "sem_xT")
        sem_xTo = Sem(nc, es, "sem_xTo")
        sem_wl = [Sem(nc, es, f"sem_wl{i}") for i in range(2)]
        sem_ts = [Sem(nc, es, f"sem_ts{i}") for i in range(2)]
        sem_wT = Sem(nc, es, "sem_wT")
        sem_xr = [Sem(nc, es, f"sem_xr{i}") for i in range(3)]
        sem_x1s = [Sem(nc, es, f"sem_x1s{i}") for i in range(2)]
        sem_y = [Sem(nc, es, f"sem_y{i}") for i in range(3)]
        sem_op = [Sem(nc, es, f"sem_op{i}") for i in range(2)]
        sem_wq = [Sem(nc, es, f"sem_wq{i}") for i in range(2)]
        sem_kv = [Sem(nc, es, f"sem_kv{i}") for i in range(2)]
        pe = Sem(nc, es, "pe")
        act = Sem(nc, es, "act")
        dve = Sem(nc, es, "dve")
        gp = Sem(nc, es, "gp")

        block = es.enter_context(nc.Block())
        PE = block.tensor
        ACTE = block.scalar
        DVE = block.vector
        GPS = block.gpsimd
        SYN = block.sync

        headoutT = s_att2a.enter_context(
            nc.sbuf_tensor("headoutT", [128, 8 * T_OWN], BF16, side="right"))
        xT = SB(s_att, "xT", [128, 8 * T_TOT], BF16)
        QT = SB(s_att, "QT", [128, 8 * T_OWN], BF16)
        k_pair = [SB(s_att, f"k_pair{i}", [128, T_TOT], BF16) for i in range(2)]
        v_aug = [SB(s_att, f"v_aug{i}", [128, 32 * 130 + 64], BF16) for i in range(2)]
        vfm = [SB(s_att, f"vfm{i}", [128, 512], BF16) for i in range(2)]
        probs = [SB(s_att, f"probs{i}", [128, 1024], BF16) for i in range(2)]
        sqb = [SB(s_att, f"sqb{i}", [128, 512], BF16) for i in range(2)]
        wkv = [SB(s_att, f"wkv{i}", [128, 8 * 256], BF16) for i in range(2)]
        wqm = [SB(s_att, f"wqm{i}", [128, 8 * 128], BF16) for i in range(2)]
        rec = SB(s_att, "rec", [64, 1024], F32)
        den2 = SB(s_att, "den2", [1, 1024], BF16)
        raw = [SB(s_att, f"raw{i}", [128, 512], BF16) for i in range(4)]
        wstage = SB(s_att, "wstage", [128, 1024], F32)
        tern_a = SB(s_att, "tern_a", [128, 1024], BF16)
        tern_b = SB(s_att, "tern_b", [128, 1024], BF16)
        tern_c = SB(s_att, "tern_c", [128, 1024], BF16)

        ps_s = [PS(s_att, f"ps_s{i}", [128, 1024], F32) for i in range(2)]
        ps_oA = PS(s_att, "ps_oA", [128, 512], F32)
        ps_oB = PS(s_att, "ps_oB", [128, 512], F32)
        ps_x = PS(s_att, "ps_x", [128, 512], F32)
        ps_t = PS(s_att, "ps_t", [128, 1024], BF16)

        psx_free = [(dve, 0)]

        def vsync(v, ins):
            dve.inc(ins)
            dve.wait(v, dve.n)

        def asyncb(sc, ins):
            act.inc(ins)
            act.wait_planned(sc, act.n)

        def psx_acquire(eng):
            s, v = psx_free[0]
            s.wait(eng, v)

        pst_free = [(dve, 0)]

        def _(g):
            g.memset(ident[:, :], 0.0)
            g.affine_select(out=ident[:, :], in_=ident[:, :],
                            compare_op=ALU.not_equal, fill=1.0, base=0,
                            pattern=[[-1, 128]], channel_multiplier=1)
            g.memset(ones_bf[:, :], 1.0)
            g.memset(ones_bf[64:128, 1:2], 0.0)
            ins = g.memset(onesrow_bf[:, :], 1.0)
            gp.inc(ins)
            g.dma_start(out=xbf_d[0:T_OWN, :], in_=x_own[:, :]).then_inc(sem_xb.h, 16)
            g.dma_start(out=xbf_d[T_OWN:T_TOT, :], in_=x_oth[:, :]).then_inc(sem_xb.h, 16)
            g.dma_start(out=b_out_bf[0:1, :], in_=b_out[None, :]).then_inc(sem_xb.h, 16)
        GPS(_)
        sem_xb.n += 48
        m_ident = gp.n

        def _(v):
            v.memset(ones_fc[:, :], 1.0)
            v.memset(ones_f[:, :], 1.0)
            v.memset(negones_f[:, :], -1.0)
            v.memset(acc_w[:, :], 0.0)
            v.memset(sml[:, :], 0.0)
            v.memset(vec2[:, :], 0.0)
            v.memset(vec[:, :], 0.0)
            for i in range(2):
                for kt in range(32):
                    v.memset(v_aug[i][:, kt * 130 + 64:kt * 130 + 65], 1.0)
                    v.memset(v_aug[i][:, kt * 130 + 129:kt * 130 + 130], 1.0)
            ins = v.memset(vec2[:, 30:31], EPS_LN)
            dve.inc(ins)
        DVE(_)
        m_prelude = dve.n

        def _(s):
            s.wait_ge(sem_xb.h, 16)
            for c in range(8):
                s.dma_start(out=xT[:, c * T_TOT:c * T_TOT + T_OWN],
                            in_=xbf_d[0:T_OWN, c * 128:(c + 1) * 128],
                            transpose=True).then_inc(sem_xTo.h, 16)
            s.wait_ge(sem_xb.h, 32)
            for c in range(8):
                s.dma_start(out=xT[:, c * T_TOT + T_OWN:(c + 1) * T_TOT],
                            in_=xbf_d[T_OWN:T_TOT, c * 128:(c + 1) * 128],
                            transpose=True).then_inc(sem_xT.h, 16)
        SYN(_)
        sem_xTo.n += 8 * 16
        sem_xT.n += 8 * 16

        m_wq = [0] * 8
        m_wq_pe = [0] * 8

        def emit_wq_load(mm):
            def _(g):
                if mm >= 2:
                    pe.wait(g, m_wq_pe[mm - 2])
                for k in range(8):
                    g.dma_start(out=wqm[mm % 2][:, k * 128:(k + 1) * 128],
                                in_=w_qkv[k * 128:(k + 1) * 128,
                                          mm * 128:(mm + 1) * 128]
                                ).then_inc(sem_wq[mm % 2].h, 16)
            GPS(_)
            sem_wq[mm % 2].n += 128

        emit_wq_load(0)
        emit_wq_load(1)

        m_qt = [0] * 10
        m_qev = {}
        sqb_free = [(pe, 0), (pe, 0)]

        wq_queue = []
        m_wst_load = {}
        m_wst_dve = {}
        m_tern_dve = {}
        m_tern_st = {}

        def w_tile_src(t):
            if t < 32:
                return w1[t * 128:(t + 1) * 128, :]
            tt = t - 32
            return w2[(tt % 8) * 128:(tt % 8) * 128 + 128,
                      (tt // 8) * 1024:(tt // 8) * 1024 + 1024]

        def w_tile_dst(t):
            if t < 32:
                return w1t_d[t * 128:(t + 1) * 128, :]
            tt = t - 32
            return w2t_d[(tt % 8) * 128:(tt % 8) * 128 + 128,
                         (tt // 8) * 1024:(tt // 8) * 1024 + 1024]

        def mk_stats_load(t):
            def emit():
                def _(s):
                    if t >= 1:
                        dve.wait(s, m_wst_dve[t - 1])
                    s.dma_start(out=wstage[:, :],
                                in_=w_tile_src(t)).then_inc(sem_wl[t % 2].h, 16)
                SYN(_)
                sem_wl[t % 2].n += 16
            return emit

        def mk_stats_red(t):
            def emit():
                def _(v):
                    sem_wl[t % 2].wait_all(v)
                    col = 0 if t < 32 else 1
                    ins = v.reduce_sum(acc_w[:, 2:3], wstage[:, :],
                                       axis=AX.X, apply_absolute_value=True)
                    vsync(v, ins)
                    ins = v.tensor_add(acc_w[:, col:col + 1], acc_w[:, col:col + 1],
                                       acc_w[:, 2:3])
                    dve.inc(ins)
                DVE(_)
                m_wst_dve[t] = dve.n
            return emit

        m_thr = [0]

        def mk_s_finalize():
            def emit():
                def _(t):
                    dve.wait(t, m_wst_dve[63])
                    psx_acquire(t)
                    t.matmul(ps_x[0:1, 0:1], acc_w[:, 0:1], ones_fc[:, 0:1],
                             start=True, stop=True)
                    ins = t.matmul(ps_x[0:1, 1:2], acc_w[:, 1:2], ones_fc[:, 0:1],
                                   start=True, stop=True)
                    pe.inc(ins)
                PE(_)
                m1 = pe.n

                def _(v):
                    pe.wait(v, m1)
                    v.tensor_scalar(sml[0:1, 37:38], ps_x[0:1, 0:1],
                                    1.0 / (HID * D), None, ALU.mult)
                    ins = v.tensor_scalar(sml[0:1, 38:39], ps_x[0:1, 1:2],
                                          1.0 / (HID * D), None, ALU.mult)
                    vsync(v, ins)
                    v.tensor_scalar_max(sml[0:1, 37:38], sml[0:1, 37:38], EPS_Q)
                    ins = v.tensor_scalar_max(sml[0:1, 38:39], sml[0:1, 38:39],
                                              EPS_Q)
                    vsync(v, ins)
                    v.tensor_scalar_mul(sml[0:1, 39:40], sml[0:1, 37:38], 0.5)
                    ins = v.tensor_scalar_mul(sml[0:1, 40:41], sml[0:1, 38:39], 0.5)
                    dve.inc(ins)
                DVE(_)
                m2 = dve.n
                psx_free[0] = (dve, m2)

                def _(t):
                    dve.wait(t, m2)
                    psx_acquire(t)
                    t.matmul(ps_x[:, 0:1], ones_f[0:1, :], sml[0:1, 39:40],
                             start=True, stop=True)
                    t.matmul(ps_x[:, 1:2], ones_f[0:1, :], sml[0:1, 40:41],
                             start=True, stop=True)
                    t.matmul(ps_x[:, 2:3], ones_f[0:1, :], sml[0:1, 37:38],
                             start=True, stop=True)
                    ins = t.matmul(ps_x[:, 3:4], ones_f[0:1, :], sml[0:1, 38:39],
                                   start=True, stop=True)
                    pe.inc(ins)
                PE(_)
                m3 = pe.n

                def _(v):
                    pe.wait(v, m3)
                    v.tensor_copy(vec[:, 2:3], ps_x[:, 0:1])
                    v.tensor_scalar_mul(vec[:, 3:4], ps_x[:, 0:1], -1.0)
                    v.tensor_copy(vec[:, 4:5], ps_x[:, 1:2])
                    v.tensor_scalar_mul(vec[:, 5:6], ps_x[:, 1:2], -1.0)
                    v.tensor_copy(vec[:, 6:7], ps_x[:, 2:3])
                    ins = v.tensor_copy(vec[:, 7:8], ps_x[:, 3:4])
                    dve.inc(ins)
                DVE(_)
                psx_free[0] = (dve, dve.n)
                m_thr[0] = dve.n
            return emit

        def mk_tern_load(t):
            def emit():
                def _(s):
                    if t >= 1:
                        dve.wait(s, m_tern_dve[t - 1])
                    else:
                        dve.wait(s, m_wst_dve[63])
                    s.dma_start(out=wstage[:, :],
                                in_=w_tile_src(t)).then_inc(sem_wl[t % 2].h, 16)
                SYN(_)
                sem_wl[t % 2].n += 16
            return emit

        def mk_tern_comp(t):
            def emit():
                def _(v):
                    sem_wl[t % 2].wait_all(v)
                    if t >= 1:
                        sem_ts[(t - 1) % 2].wait_all(v)
                    w = wstage[:, :]
                    thr = vec[:, 2:3] if t < 32 else vec[:, 4:5]
                    nthr = vec[:, 3:4] if t < 32 else vec[:, 5:6]
                    v.tensor_scalar(tern_a[:, :], w, thr, None, ALU.is_ge)
                    ins = v.tensor_scalar(tern_b[:, :], w, nthr, None, ALU.is_le)
                    vsync(v, ins)
                    ins = v.tensor_sub(tern_c[:, 0:1024],
                                       tern_a[:, :], tern_b[:, :])
                    dve.inc(ins)
                DVE(_)
                m_tern_dve[t] = dve.n

                def _(s):
                    dve.wait(s, m_tern_dve[t])
                    s.dma_start(out=w_tile_dst(t),
                                in_=tern_c[:, 0:1024]
                                ).then_inc(sem_ts[t % 2].h, 16)
                SYN(_)
                sem_ts[t % 2].n += 16
            return emit

        for t in range(64):
            wq_queue.append(mk_stats_load(t))
            wq_queue.append(mk_stats_red(t))
        wq_queue.append(mk_s_finalize())
        for t in range(64):
            wq_queue.append(mk_tern_load(t))
            wq_queue.append(mk_tern_comp(t))

        m_kv = [0] * 10
        m_negM = [0] * 10
        m_pair_pe_done = [0] * 10
        m_pair_act_done = [0] * 10
        m_kvproj_done = [0] * 10

        def emit_wkv(p):
            pb = p % 2
            def _(g):
                if p >= 2:
                    pe.wait(g, m_kvproj_done[p - 2])
                for k in range(8):
                    g.dma_start(
                        out=wkv[pb][:, k * 256:k * 256 + 128],
                        in_=w_qkv[k * 128:(k + 1) * 128,
                                  D + p * 128:D + p * 128 + 128]
                    ).then_inc(sem_kv[pb].h, 16)
                    g.dma_start(
                        out=wkv[pb][:, k * 256 + 128:k * 256 + 256],
                        in_=w_qkv[k * 128:(k + 1) * 128,
                                  2 * D + p * 128:2 * D + p * 128 + 128]
                    ).then_inc(sem_kv[pb].h, 16)
            GPS(_)
            sem_kv[pb].n += 256

        def make_prep(p, A0ref):
            ops = []
            pb = p % 2

            def mk_qproj(n):
                def emit():
                    def _(t):
                        if n == 0:
                            sem_wq[p % 2].wait_all(t)
                            if p == 0:
                                sem_xTo.wait_all(t)
                                gp.wait(t, m_ident)
                        psx_acquire(t)
                        ins = None
                        for k in range(8):
                            ins = t.matmul(
                                ps_x[:, :],
                                wqm[p % 2][:, k * 128:(k + 1) * 128],
                                xT[:, k * T_TOT + n * 512:
                                   k * T_TOT + n * 512 + 512],
                                start=(k == 0), stop=(k == 7))
                        pe.inc(ins)
                    PE(_)
                    m_mm = pe.n
                    m_wq_pe[p] = pe.n

                    def _(v):
                        pe.wait(v, m_mm)
                        ins = v.tensor_copy(
                            QT[:, p * T_OWN + n * 512:p * T_OWN + n * 512 + 512],
                            ps_x[:, :])
                        dve.inc(ins)
                    DVE(_)
                    psx_free[0] = (dve, dve.n)
                    m_qev[(p, n)] = dve.n
                    if n == 3:
                        m_qt[p] = dve.n
                return emit
            for n in range(4):
                ops.append(mk_qproj(n))

            def em_wqload():
                if p + 2 < NPAIR:
                    emit_wq_load(p + 2)
            ops.append(em_wqload)

            def mk_qnorm(j):
                def emit():
                    def _(g):
                        dve.wait(g, m_qev[(p, j)])
                        s_, v_ = sqb_free[j % 2]
                        s_.wait(g, v_)
                        sl = QT[:, p * T_OWN + j * 512:p * T_OWN + j * 512 + 512]
                        ins = g.tensor_tensor(sqb[j % 2][:, :], sl, sl, ALU.mult)
                        gp.inc(ins)
                    GPS(_)
                    m_sq = gp.n

                    def _(t):
                        gp.wait(t, m_sq)
                        psx_acquire(t)
                        t.matmul(ps_x[0:1, :], ones_bf[0:64, 0:1],
                                 sqb[j % 2][0:64, :], start=True, stop=True,
                                 tile_position=(0, 0))
                        ins = t.matmul(ps_x[32:33, :], ones_bf[64:128, 0:1],
                                       sqb[j % 2][64:128, :],
                                       start=True, stop=True,
                                       tile_position=(64, 32))
                        pe.inc(ins)
                    PE(_)
                    m_nm = pe.n
                    sqb_free[j % 2] = (pe, pe.n)

                    def _(v):
                        pe.wait(v, m_nm)
                        v.reduce_max(sml[0:1, 48 + j:49 + j], ps_x[0:1, :],
                                     axis=AX.X)
                        ins = v.reduce_max(sml[0:1, 52 + j:53 + j],
                                           ps_x[32:33, :], axis=AX.X)
                        dve.inc(ins)
                    DVE(_)
                    psx_free[0] = (dve, dve.n)
                return emit
            for j in range(4):
                ops.append(mk_qnorm(j))

            def em_qfin():
                def _(v):
                    dve.wait(v, dve.n)
                    v.reduce_max(sml[0:1, 2 * p:2 * p + 1], sml[0:1, 48:52],
                                 axis=AX.X)
                    ins = v.reduce_max(sml[0:1, 2 * p + 1:2 * p + 2],
                                       sml[0:1, 52:56], axis=AX.X)
                    dve.inc(ins)
                DVE(_)
            ops.append(em_qfin)

            def em_wkv():
                def _(g):
                    if p >= 2:
                        pe.wait(g, m_pair_pe_done[p - 2])
                    for k in range(8):
                        g.dma_start(
                            out=wkv[pb][:, k * 256:k * 256 + 128],
                            in_=w_qkv[k * 128:(k + 1) * 128,
                                      D + p * 128:D + p * 128 + 128]
                        ).then_inc(sem_kv[pb].h, 16)
                        g.dma_start(
                            out=wkv[pb][:, k * 256 + 128:k * 256 + 256],
                            in_=w_qkv[k * 128:(k + 1) * 128,
                                      2 * D + p * 128:2 * D + p * 128 + 128]
                        ).then_inc(sem_kv[pb].h, 16)
                GPS(_)
                sem_kv[pb].n += 256

            def mk_kchunk(n):
                def emit():
                    def _(t):
                        sem_kv[pb].wait_all(t)
                        if p < 2 and n == 0:
                            sem_xTo.wait_all(t)
                            sem_xT.wait_all(t)
                        psx_acquire(t)
                        ins = None
                        for k in range(8):
                            ins = t.matmul(
                                ps_x[:, :],
                                wkv[pb][:, k * 256:k * 256 + 128],
                                xT[:, k * T_TOT + n * 512:k * T_TOT + n * 512 + 512],
                                start=(k == 0), stop=(k == 7))
                        pe.inc(ins)
                    PE(_)
                    m_mm = pe.n

                    def _(v):
                        pe.wait(v, m_mm)
                        if p >= 2 and n == 0:
                            pe.wait(v, m_pair_pe_done[p - 2])
                        ins = v.tensor_copy(
                            k_pair[pb][:, n * 512:(n + 1) * 512], ps_x[:, :])
                        dve.inc(ins)
                    DVE(_)
                    psx_free[0] = (dve, dve.n)
                    mk_done[n] = dve.n
                return emit
            mk_done = {}

            def mk_vchunk(n):
                st = {}

                def emit1():
                    def _(t):
                        psx_acquire(t)
                        ins = None
                        for k in range(8):
                            ins = t.matmul(
                                ps_x[:, :],
                                wkv[pb][:, k * 256 + 128:k * 256 + 256],
                                xT[:, k * T_TOT + n * 512:k * T_TOT + n * 512 + 512],
                                start=(k == 0), stop=(k == 7))
                        pe.inc(ins)
                    PE(_)
                    m_mm = pe.n

                    def _(v):
                        pe.wait(v, m_mm)
                        ins = v.tensor_copy(vfm[n % 2][:, :], ps_x[:, :])
                        dve.inc(ins)
                    DVE(_)
                    st["ev"] = dve.n
                    psx_free[0] = (dve, dve.n)

                def emit2():
                    def _(t):
                        dve.wait(t, st["ev"])
                        sm, vv = pst_free[0]
                        sm.wait(t, vv)
                        ins = None
                        for j in range(4):
                            ins = t.transpose(ps_t[:, j * 128:(j + 1) * 128],
                                              vfm[n % 2][:, j * 128:(j + 1) * 128],
                                              ident[:, :])
                        pe.inc(ins)
                    PE(_)
                    m_tr = pe.n

                    def _(v):
                        pe.wait(v, m_tr)
                        if p >= 2 and n == 0:
                            pe.wait(v, m_pair_pe_done[p - 2])
                        ins = None
                        for j in range(4):
                            kt = n * 4 + j
                            v.tensor_copy(
                                v_aug[pb][:, kt * 130:kt * 130 + 64],
                                ps_t[:, j * 128:j * 128 + 64])
                            ins = v.tensor_copy(
                                v_aug[pb][:, kt * 130 + 65:kt * 130 + 129],
                                ps_t[:, j * 128 + 64:j * 128 + 128])
                        dve.inc(ins)
                    DVE(_)
                    pst_free[0] = (dve, dve.n)
                    if n == 7:
                        m_kv[p] = dve.n
                        m_kvproj_done[p] = pe.n
                return [emit1, emit2]
            for n in range(8):
                ops.append(mk_kchunk(n))
                for c in mk_vchunk(n):
                    ops.append(c)

            mk_nm = {}

            def mk_knorm(n):
                def emit():
                    def _(g):
                        dve.wait(g, mk_done[n])
                        s_, v_ = sqb_free[n % 2]
                        s_.wait(g, v_)
                        sl = k_pair[pb][:, n * 512:(n + 1) * 512]
                        ins = g.tensor_tensor(sqb[n % 2][:, :], sl, sl, ALU.mult)
                        gp.inc(ins)
                    GPS(_)
                    m_sq = gp.n

                    def _(t):
                        gp.wait(t, m_sq)
                        psx_acquire(t)
                        t.matmul(ps_x[0:1, :], ones_bf[0:64, 0:1],
                                 sqb[n % 2][0:64, :], start=True, stop=True,
                                 tile_position=(0, 0))
                        ins = t.matmul(ps_x[32:33, :], ones_bf[64:128, 0:1],
                                       sqb[n % 2][64:128, :], start=True, stop=True,
                                       tile_position=(64, 32))
                        pe.inc(ins)
                    PE(_)
                    m_nm = pe.n
                    sqb_free[n % 2] = (pe, pe.n)

                    def _(v):
                        pe.wait(v, m_nm)
                        v.reduce_max(sml[0:1, 16 + n:17 + n], ps_x[0:1, :],
                                     axis=AX.X)
                        ins = v.reduce_max(sml[0:1, 24 + n:25 + n], ps_x[32:33, :],
                                           axis=AX.X)
                        dve.inc(ins)
                    DVE(_)
                    psx_free[0] = (dve, dve.n)
                return emit
            for n in range(8):
                ops.append(mk_knorm(n))

            def em_negM():
                def _(v):
                    dve.wait(v, dve.n)
                    v.reduce_max(sml[0:1, 32:33], sml[0:1, 16:24], axis=AX.X)
                    ins = v.reduce_max(sml[0:1, 33:34], sml[0:1, 24:32], axis=AX.X)
                    vsync(v, ins)
                    v.tensor_tensor(sml[0:1, 34:35], sml[0:1, 32:33],
                                    sml[0:1, 2 * p:2 * p + 1], ALU.add)
                    ins = v.tensor_tensor(sml[0:1, 35:36], sml[0:1, 33:34],
                                          sml[0:1, 2 * p + 1:2 * p + 2], ALU.add)
                    vsync(v, ins)
                    ins = v.tensor_tensor(sml[0:1, 36:37], sml[0:1, 34:35],
                                          sml[0:1, 35:36], ALU.max)
                    dve.inc(ins)
                DVE(_)
                m_t = dve.n

                def _(t):
                    dve.wait(t, m_t)
                    psx_acquire(t)
                    ins = t.matmul(ps_x[:, 0:1], negones_f[0:1, :],
                                   sml[0:1, 36:37], start=True, stop=True)
                    pe.inc(ins)
                PE(_)
                m_b = pe.n

                def _(v):
                    pe.wait(v, m_b)
                    if p >= 2:
                        act.wait_planned(v, m_pair_act_done[p - 2])
                    ins = v.tensor_scalar(vec[:, pb:pb + 1], ps_x[:, 0:1],
                                          1.0 / 16.0, None, ALU.mult)
                    dve.inc(ins)
                DVE(_)
                psx_free[0] = (dve, dve.n)
                m_negM[p] = dve.n
            ops.append(em_negM)
            return ops

        emit_wkv(0)
        emit_wkv(1)
        for emitfn in make_prep(0, 0):
            emitfn()
        prep_queue = list(make_prep(1, 0))

        A0 = act.n
        m_no = {}
        m_norm_done = {}
        raw_free = [(dve, 0), (dve, 0)]
        m_bc_prev = [0]
        norm_queue = []
        norms_emitted = [0]
        exp_qblks = []

        for p in range(NPAIR):
            pb = p % 2
            if p + 2 < NPAIR:
                emit_wkv(p + 2)
            for qb in range(NQB):
                gbase = (p * NQB + qb) * NKC
                for kc0 in range(0, NKC, 2):
                    def _(t, p=p, qb=qb, kc0=kc0, pb=pb):
                        for kc in (kc0, kc0 + 1):
                            G = (p * NQB + qb) * NKC + kc
                            sb_ = G % 2
                            if kc == 0:
                                dve.wait(t, m_kv[p])
                                if qb == 0:
                                    dve.wait(t, m_qt[p])
                            act.wait_planned(t, A0 + G - 1)
                            t.matmul(ps_s[sb_][:, 0:512],
                                     k_pair[pb][0:64, kc * 128:(kc + 1) * 128],
                                     QT[0:64, p * T_OWN + qb * 512:
                                        p * T_OWN + qb * 512 + 512],
                                     start=True, stop=True, tile_position=(0, 0))
                            ins = t.matmul(ps_s[sb_][:, 512:1024],
                                           k_pair[pb][64:128,
                                                      kc * 128:(kc + 1) * 128],
                                           QT[64:128, p * T_OWN + qb * 512:
                                              p * T_OWN + qb * 512 + 512],
                                           start=True, stop=True,
                                           tile_position=(64, 0))
                            pe.inc(ins)
                            m_sc[G] = pe.n
                            if kc >= 1:
                                Gp = G - 1
                                kcp = Gp % NKC
                                spb = Gp % 2
                                if kcp == 0:
                                    if qb > 0:
                                        dve.wait(t, m_no[(p, qb - 1)])
                                    elif p >= 1:
                                        dve.wait(t, m_no[(p - 1, NQB - 1)])
                                act.wait_planned(t, A0 + Gp + 1)
                                t.matmul(ps_oA[:, :],
                                         v_aug[pb][:, kcp * 130:kcp * 130 + 128],
                                         probs[spb][:, 0:512],
                                         start=(kcp == 0), stop=False)
                                ins = t.matmul(ps_oB[:, :],
                                               v_aug[pb][:, kcp * 130 + 65:
                                                          kcp * 130 + 193],
                                               probs[spb][:, 512:1024],
                                               start=(kcp == 0), stop=False)
                                pe.inc(ins)
                                m_pv[Gp] = pe.n
                    PE(_)
                    budget = 2
                    while budget > 0 and prep_queue:
                        prep_queue.pop(0)()
                        budget -= 1
                    if len(norm_queue) >= 2:
                        norm_queue.pop(0)()
                    elif budget > 0 and wq_queue:
                        wq_queue.pop(0)()

                exp_qblks.append((p, qb, pb))
                if qb == NQB - 1:
                    def _(sc, batch=list(exp_qblks)):
                        first = True
                        for (pp, qq, ppb) in batch:
                            for kc in range(NKC):
                                G = (pp * NQB + qq) * NKC + kc
                                sb_ = G % 2
                                if first:
                                    dve.wait(sc, m_negM[pp])
                                    first = False
                                pe.wait(sc, m_sc[G])
                                if G >= 2 and (G - 2) in m_pv:
                                    pe.wait(sc, m_pv[G - 2])
                                ins = sc.activation(probs[sb_][:, :],
                                                    ps_s[sb_][:, :], AF.Exp,
                                                    bias=vec[:, ppb:ppb + 1],
                                                    scale=0.125)
                                act.inc(ins)
                    ACTE(_)
                    exp_qblks.clear()
                    assert act.n == A0 + (p * NQB + NQB) * NKC, (act.n, A0)

                def _(t, p=p, qb=qb, pb=pb):
                    G = (p * NQB + qb) * NKC + (NKC - 1)
                    kcp = NKC - 1
                    act.wait_planned(t, A0 + G + 1)
                    t.matmul(ps_oA[:, :], v_aug[pb][:, kcp * 130:kcp * 130 + 128],
                             probs[G % 2][:, 0:512], start=False, stop=True)
                    ins = t.matmul(ps_oB[:, :],
                                   v_aug[pb][:, kcp * 130 + 65:kcp * 130 + 193],
                                   probs[G % 2][:, 512:1024], start=False,
                                   stop=True)
                    pe.inc(ins)
                    m_pv[G] = pe.n
                PE(_)
                if qb == NQB - 1:
                    m_pair_pe_done[p] = pe.n
                    m_pair_act_done[p] = A0 + (p + 1) * NQB * NKC

                rb = (2 * (p * NQB + qb)) % 4
                gq = p * NQB + qb
                assert gq < 2 or norms_emitted[0] >= gq - 1, \
                    (gq, norms_emitted[0])
                def _(v, p=p, qb=qb, rb=rb):
                    pe.wait(v, m_pv[(p * NQB + qb) * NKC + NKC - 1])
                    s_, v_ = raw_free[rb // 2]
                    s_.wait(v, v_)
                    v.tensor_copy(raw[rb][:, :], ps_oA[:, :])
                    ins = v.tensor_copy(raw[rb + 1][:, :], ps_oB[:, :])
                    dve.inc(ins)
                DVE(_)
                m_no[(p, qb)] = dve.n

                def mk_norm(p=p, qb=qb, rb=rb):
                    def emit():
                        def _(v):
                            dve.wait(v, m_no[(p, qb)])
                            pe.wait(v, m_bc_prev[0])
                            v.tensor_copy(den2[0:1, 0:512], raw[rb][64:65, :])
                            ins = v.tensor_copy(den2[0:1, 512:1024],
                                                raw[rb + 1][64:65, :])
                            dve.inc(ins)
                        DVE(_)
                        m_dc = dve.n

                        def _(t):
                            dve.wait(t, m_dc)
                            psx_acquire(t)
                            t.matmul(ps_x[0:64, :], onesrow_bf[0:1, 0:64],
                                     den2[0:1, 0:512], start=True, stop=True,
                                     tile_position=(0, 0))
                            ins = t.matmul(ps_x[64:128, :], onesrow_bf[0:1, 0:64],
                                           den2[0:1, 512:1024], start=True,
                                           stop=True, tile_position=(0, 64))
                            pe.inc(ins)
                        PE(_)
                        m_bc = pe.n
                        m_bc_prev[0] = m_bc

                        def _(v):
                            pe.wait(v, m_bc)
                            v.reciprocal(rec[0:64, 0:512], ps_x[0:64, :])
                            ins = v.reciprocal(rec[0:64, 512:1024],
                                               ps_x[64:128, :])
                            vsync(v, ins)
                            col = p * T_OWN + qb * 512
                            v.tensor_tensor(headoutT[0:64, col:col + 512],
                                            raw[rb][0:64, :], rec[0:64, 0:512],
                                            ALU.mult)
                            ins = v.tensor_tensor(headoutT[64:128, col:col + 512],
                                                  raw[rb + 1][0:64, :],
                                                  rec[0:64, 512:1024], ALU.mult)
                            dve.inc(ins)
                        DVE(_)
                        psx_free[0] = (dve, dve.n)
                        raw_free[rb // 2] = (dve, dve.n)
                        m_norm_done[(p, qb)] = dve.n
                        norms_emitted[0] += 1
                    return emit
                norm_queue.append(mk_norm())

            while prep_queue:
                prep_queue.pop(0)()
            if p + 2 < NPAIR:
                prep_queue = list(make_prep(p + 2, A0))

        while norm_queue:
            norm_queue.pop(0)()
        while wq_queue:
            wq_queue.pop(0)()

        m_attn_pe = pe.n
        m_attn_dve = dve.n
        m_attn_act = act.n
        m_headout = max(m_norm_done.values())

        if debug:
            def _(s):
                dve.wait(s, m_headout)
                s.dma_start(out=ho_d[:, :], in_=headoutT[:, :]).then_inc(sem_s.h, 16)
            SYN(_)
            sem_s.n += 16

        s_att.close()

        w1T = SB(s_mid, "w1T", [128, 8 * 4096], BF16)
        w2T = SB(s_mid, "w2T", [128, 32 * 1024], BF16)
        wout_sb = SB(s_2a, "wout_sb", [128, 8 * 1024], BF16)
        x_res = SB(s_2a, "x_res", [128, 3 * 1024], F32)
        r1b = SB(s_2a, "r1b", [128, 2 * 1024], F32)
        x1st = SB(s_2a, "x1st", [128, 1024], F32)
        ps_a = [PS(s_2a, f"ps_a{i}", [128, 1024], F32) for i in range(2)]

        def _(g):
            pe.wait(g, m_attn_pe)
            dve.wait(g, m_attn_dve)
            act.wait_planned(g, m_attn_act)
            sem_ts[0].wait_all(g)
            sem_ts[1].wait_all(g)
            for c in range(8):
                g.dma_start(out=wout_sb[:, c * 1024:(c + 1) * 1024],
                            in_=w_out[c * 128:(c + 1) * 128, :]).then_inc(sem_g.h, 16)
        GPS(_)
        sem_g.n += 128
        m_wout = sem_g.n

        def _(s):
            pe.wait(s, m_attn_pe)
            dve.wait(s, m_attn_dve)
            sem_ts[0].wait_all(s)
            sem_ts[1].wait_all(s)
            for c in range(8):
                s.dma_start(out=w1T[:, c * 4096:(c + 1) * 4096],
                            in_=w1t_d[:, c * 128:(c + 1) * 128],
                            transpose=True).then_inc(sem_wT.h, 16)
        SYN(_)
        sem_wT.n += 128
        m_w1T = sem_wT.n
        def _(s):
            sem_ts[0].wait_all(s)
            sem_ts[1].wait_all(s)
            for c in range(32):
                s.dma_start(out=w2T[:, c * 1024:(c + 1) * 1024],
                            in_=w2t_d[:, c * 128:(c + 1) * 128],
                            transpose=True).then_inc(sem_wT.h, 16)
        SYN(_)
        sem_wT.n += 32 * 16
        m_w2T = sem_wT.n

        m_r1_dve = [0] * NSLAB
        m_ln1_act = [0] * NSLAB
        m_rstd1 = [0] * NSLAB
        m_sqa1 = {}

        def emit_ln1_norm(sc, s_i):
            dve.wait(sc, m_rstd1[s_i])
            if s_i >= 1:
                sem_x1s[(s_i - 1) % 2].wait_all(sc)
            b = 8 * (s_i % 2)
            r1 = r1b[:, (s_i % 2) * 1024:(s_i % 2) * 1024 + 1024]
            ins = sc.activation(
                x1st[:, :], r1,
                AF.Identity, scale=vec2[:, b + 4:b + 5],
                bias=vec2[:, b + 5:b + 6])
            act.inc(ins)
            m_ln1_act[s_i] = act.n

        def emit_x1_store(s_i):
            def _(s, s_i=s_i):
                act.wait_planned(s, m_ln1_act[s_i])
                s.dma_start(out=x1_d[s_i * 128:(s_i + 1) * 128, :],
                            in_=x1st[:, :]).then_inc(sem_x1s[s_i % 2].h, 16)
            SYN(_)
            sem_x1s[s_i % 2].n += 16

        for s_i in range(NSLAB):
            def _(s, s_i=s_i):
                if s_i == 0:
                    pe.wait(s, m_attn_pe)
                    dve.wait(s, m_attn_dve)
                if s_i >= 3:
                    dve.wait(s, m_r1_dve[s_i - 3])
                s.dma_start(out=x_res[:, (s_i % 3) * 1024:(s_i % 3) * 1024 + 1024],
                            in_=x_own[s_i * 128:(s_i + 1) * 128, :]
                            ).then_inc(sem_xr[s_i % 3].h, 16)
            SYN(_)
            sem_xr[s_i % 3].n += 16

            def _(t, s_i=s_i):
                if s_i == 0:
                    sem_g.wait(t, m_wout)
                    sem_xb.wait(t, 48)
                    dve.wait(t, m_headout)
                if s_i >= 2:
                    dve.wait(t, m_r1_dve[s_i - 2])
                pa = ps_a[s_i % 2]
                for half in range(2):
                    for c in range(8):
                        t.matmul(
                            pa[:, half * 512:half * 512 + 512],
                            headoutT[:, c * T_OWN + s_i * 128:
                                     c * T_OWN + s_i * 128 + 128],
                            wout_sb[:, c * 1024 + half * 512:
                                    c * 1024 + half * 512 + 512],
                            start=(c == 0), stop=False)
                    ins = t.matmul(pa[:, half * 512:half * 512 + 512],
                                   onesrow_bf[0:1, :],
                                   b_out_bf[0:1, half * 512:half * 512 + 512],
                                   start=False, stop=True)
                    pe.inc(ins)
            PE(_)
            m_op = pe.n

            def _(v, s_i=s_i, m_op=m_op):
                pe.wait(v, m_op)
                sem_xr[s_i % 3].wait_all(v)
                if s_i >= 2:
                    act.wait_planned(v, m_ln1_act[s_i - 2])
                if s_i >= 1 and (s_i - 1) in m_sqa1:
                    act.wait_planned(v, m_sqa1[s_i - 1])
                b = 8 * (s_i % 2)
                r1 = r1b[:, (s_i % 2) * 1024:(s_i % 2) * 1024 + 1024]
                ins = v.scalar_tensor_tensor(
                    r1, ps_a[s_i % 2][:, :], 1.0,
                    x_res[:, (s_i % 3) * 1024:(s_i % 3) * 1024 + 1024],
                    ALU.mult, ALU.add, accum_out=vec2[:, b:b + 1])
                vsync(v, ins)
                ins = v.tensor_scalar_mul(vec2[:, b + 1:b + 2], vec2[:, b:b + 1],
                                          -1.0 / D)
                dve.inc(ins)
            DVE(_)
            m_r1 = dve.n
            m_r1_dve[s_i] = m_r1

            def _(sc, s_i=s_i, m_r1=m_r1):
                if s_i >= 1:
                    emit_ln1_norm(sc, s_i - 1)
                dve.wait(sc, m_r1)
                b = 8 * (s_i % 2)
                r1 = r1b[:, (s_i % 2) * 1024:(s_i % 2) * 1024 + 1024]
                scr1 = r1b[:, ((s_i + 1) % 2) * 1024:((s_i + 1) % 2) * 1024 + 1024]
                ins = sc.activation(scr1, r1, AF.Square,
                                    bias=vec2[:, b + 1:b + 2],
                                    accum_out=vec2[:, b + 2:b + 3])
                asyncb(sc, ins)
                ins = sc.activation(vec2[:, b + 3:b + 4], vec2[:, b + 2:b + 3],
                                    AF.Sqrt, scale=1.0 / D, bias=vec2[:, 30:31])
                act.inc(ins)
            ACTE(_)
            m_sqa = act.n
            m_sqa1[s_i] = act.n
            if s_i >= 1:
                emit_x1_store(s_i - 1)

            def _(v, s_i=s_i, m_sqa=m_sqa):
                act.wait_planned(v, m_sqa)
                b = 8 * (s_i % 2)
                ins = v.reciprocal(vec2[:, b + 4:b + 5], vec2[:, b + 3:b + 4])
                vsync(v, ins)
                ins = v.tensor_mul(vec2[:, b + 5:b + 6], vec2[:, b + 1:b + 2],
                                   vec2[:, b + 4:b + 5])
                dve.inc(ins)
            DVE(_)
            m_rstd1[s_i] = dve.n

        def _(sc):
            emit_ln1_norm(sc, NSLAB - 1)
        ACTE(_)
        emit_x1_store(NSLAB - 1)

        m_2a_pe = pe.n
        m_2a_dve = dve.n
        m_2a_act = act.n

        s_att2a.close()
        s_2a.close()

        x1b = SB(s_2b, "x1b", [128, 3 * 1024], F32)
        h1 = SB(s_2b, "h1", [128, 4096], F32)
        h1q = SB(s_2b, "h1q", [128, 4096], BF16)
        x1q = SB(s_2b, "x1q", [128, 1024], BF16)
        x1qT = SB(s_2b, "x1qT", [128, 2 * 1024], BF16)
        x2qT = SB(s_2b, "x2qT", [128, 4096], BF16)
        i32a = SB(s_2b, "i32a", [128, 1024], I32)
        i32b = SB(s_2b, "i32b", [128, 2 * 1024], I32)
        r2b = SB(s_2b, "r2b", [128, 2 * 1024], F32)
        o_sb = SB(s_2b, "o_sb", [128, 1024], F32)
        ps_h = [PS(s_2b, f"ps_h{i}", [128, 1024], F32) for i in range(2)]
        ps_f = PS(s_2b, "ps_f", [128, 1024], F32)
        ps_t2 = PS(s_2b, "ps_t2", [128, 1024], BF16)


        pst2_free = [(dve, 0)]
        m_q1v = {}
        m_gpx = {}
        m_tr1 = {}
        m_x1qT = {}
        m_gelu = {}
        m_q2 = {}
        m_q2q = {}
        m_gph = {}
        m_trh = {}
        m_x2qT = {}
        m_mm2 = {}
        m_mm1d = {}
        m_r2 = {}
        m_ln2 = {}
        m_rstd2 = {}
        m_sq2a_d = {}

        def vb_of(j):
            return 16 + 6 * (j % 2)

        def emit_stage1(j):
            def _(s, j=j):
                if j == 0:
                    pe.wait(s, m_2a_pe)
                    dve.wait(s, m_2a_dve)
                    act.wait_planned(s, m_2a_act)
                    sem_x1s[0].wait_all(s)
                    sem_x1s[1].wait_all(s)
                if j >= 3:
                    dve.wait(s, m_r2[j - 3])
                s.dma_start(out=x1b[:, (j % 3) * 1024:(j % 3) * 1024 + 1024],
                            in_=x1_d[j * 128:(j + 1) * 128, :]
                            ).then_inc(sem_y[j % 3].h, 16)
            SYN(_)
            sem_y[j % 3].n += 16

            def _(v, j=j):
                sem_y[j % 3].wait_all(v)
                if j >= 1:
                    gp.wait(v, m_gpx[j - 1])
                if j >= 2:
                    act.wait_planned(v, m_gelu[(j - 2, 3)])
                vb = vb_of(j)
                x1 = x1b[:, (j % 3) * 1024:(j % 3) * 1024 + 1024]
                ins = v.reduce_max(vec2[:, 28:29], x1, axis=AX.X,
                                   apply_absolute_value=True)
                vsync(v, ins)
                ins = v.tensor_scalar_max(vec2[:, 28:29], vec2[:, 28:29], EPS_Q)
                vsync(v, ins)
                ins = v.reciprocal(vec2[:, 29:30], vec2[:, 28:29])
                vsync(v, ins)
                v.tensor_scalar_mul(vec2[:, vb:vb + 1], vec2[:, 29:30], 127.0)
                ins = v.tensor_scalar_mul(vec2[:, 28:29], vec2[:, 28:29],
                                          1.0 / 127.0)
                vsync(v, ins)
                ins = v.tensor_mul(vec2[:, vb + 1:vb + 2], vec2[:, 28:29],
                                   vec[:, 6:7])
                vsync(v, ins)
                ins = v.tensor_scalar(i32a[:, :], x1, vec2[:, vb:vb + 1], None,
                                      ALU.mult)
                dve.inc(ins)
            DVE(_)
            m_q1v[j] = dve.n

            def _(g, j=j):
                dve.wait(g, m_q1v[j])
                if j >= 1:
                    pe.wait(g, m_tr1[j - 1])
                ins = g.tensor_copy(x1q[:, :], i32a[:, :])
                gp.inc(ins)
            GPS(_)
            m_gpx[j] = gp.n


        def emit_stage1_pe(j):
            def _(t, j=j):
                gp.wait(t, m_gpx[j])
                sm, vv = pst2_free[0]
                sm.wait(t, vv)
                ins = None
                for jj in range(8):
                    ins = t.transpose(ps_t2[:, jj * 128:(jj + 1) * 128],
                                      x1q[:, jj * 128:(jj + 1) * 128], ident[:, :])
                pe.inc(ins)
            PE(_)
            m_tr1[j] = pe.n

            def _(v, j=j):
                pe.wait(v, m_tr1[j])
                ins = v.tensor_copy(x1qT[:, (j % 2) * 1024:(j % 2) * 1024 + 1024],
                                    ps_t2[:, :])
                dve.inc(ins)
            DVE(_)
            m_x1qT[j] = dve.n
            pst2_free[0] = (dve, dve.n)

        def emit_mm1(j, part):
            for q in (0, 1) if part == 0 else (2, 3):
                def _(t, j=j, q=q):
                    if q == 0:
                        dve.wait(t, m_x1qT[j])
                        if j == 0:
                            sem_wT.wait(t, m_w1T)
                    ga = m_gelu.get((j, q - 2), m_gelu.get((j - 1, q + 2), 0))
                    act.wait_planned(t, ga)
                    ph = ps_h[q % 2]
                    xoff = (j % 2) * 1024
                    for half in range(2):
                        ins = None
                        for k in range(8):
                            ins = t.matmul(
                                ph[:, half * 512:half * 512 + 512],
                                x1qT[:, xoff + k * 128:xoff + (k + 1) * 128],
                                w1T[:, k * 4096 + q * 1024 + half * 512:
                                    k * 4096 + q * 1024 + half * 512 + 512],
                                start=(k == 0), stop=(k == 7))
                        pe.inc(ins)
                PE(_)
                m_mm1d[(j, q)] = pe.n

        def emit_act_a(j, part):
            def _(sc, j=j, part=part):
                if part == 0 and j >= 2 and (j - 2) in m_rstd2:
                    jj = j - 2
                    dve.wait(sc, m_rstd2[jj])
                    if jj >= 1:
                        sem_op[(jj - 1) % 2].wait_all(sc)
                    b = 8 * (jj % 2)
                    r2 = r2b[:, (jj % 2) * 1024:(jj % 2) * 1024 + 1024]
                    ins = sc.activation(o_sb[:, :], r2, AF.Identity,
                                        scale=vec2[:, b + 4:b + 5],
                                        bias=vec2[:, b + 5:b + 6])
                    act.inc(ins)
                    m_ln2[jj] = act.n
                for q in (0, 1) if part == 0 else (2, 3):
                    pe.wait(sc, m_mm1d[(j, q)])
                    if q == 0:
                        dve.wait(sc, m_q1v[j])
                        if j >= 1:
                            dve.wait(sc, m_q2[j - 1])
                    vb = vb_of(j)
                    ins = sc.activation(h1[:, q * 1024:(q + 1) * 1024],
                                        ps_h[q % 2][:, :], AF.Gelu,
                                        scale=vec2[:, vb + 1:vb + 2])
                    act.inc(ins)
                    m_gelu[(j, q)] = act.n
            ACTE(_)

        def emit_store(j):
            def _(s, j=j):
                act.wait_planned(s, m_ln2[j])
                s.dma_start(out=out[j * 128:(j + 1) * 128, :],
                            in_=o_sb[:, :]).then_inc(sem_op[j % 2].h, 16)
            SYN(_)
            sem_op[j % 2].n += 16

        def emit_q2conv(j):
            def _(v, j=j):
                act.wait_planned(v, m_gelu[(j, 3)])
                vb = vb_of(j)
                ins = v.reduce_max(vec2[:, 28:29], h1[:, :], axis=AX.X,
                                   apply_absolute_value=True)
                vsync(v, ins)
                ins = v.tensor_scalar_max(vec2[:, 28:29], vec2[:, 28:29], EPS_Q)
                vsync(v, ins)
                ins = v.reciprocal(vec2[:, 29:30], vec2[:, 28:29])
                vsync(v, ins)
                v.tensor_scalar_mul(vec2[:, vb + 2:vb + 3], vec2[:, 29:30], 127.0)
                ins = v.tensor_scalar_mul(vec2[:, 28:29], vec2[:, 28:29],
                                          1.0 / 127.0)
                vsync(v, ins)
                ins = v.tensor_mul(vec2[:, vb + 3:vb + 4], vec2[:, 28:29],
                                   vec[:, 7:8])
                dve.inc(ins)
            DVE(_)

            for q in range(4):
                def _(v, j=j, q=q):
                    if q < 2:
                        cg = m_gph.get((j - 1, q + 2), 0)
                    else:
                        cg = m_gph[(j, q - 2)]
                    act.wait_planned(v, cg)
                    vb = vb_of(j)
                    ins = v.tensor_scalar(
                        i32b[:, (q % 2) * 1024:(q % 2) * 1024 + 1024],
                        h1[:, q * 1024:(q + 1) * 1024],
                        vec2[:, vb + 2:vb + 3], None, ALU.mult)
                    dve.inc(ins)
                DVE(_)
                m_q2q[(j, q)] = dve.n

                def _(sc, j=j, q=q):
                    dve.wait(sc, m_q2q[(j, q)])
                    tg = m_trh.get((j - 1, q), 0)
                    pe.wait(sc, tg)
                    ins = sc.activation(
                        h1q[:, q * 1024:(q + 1) * 1024],
                        i32b[:, (q % 2) * 1024:(q % 2) * 1024 + 1024], AF.Copy)
                    act.inc(ins)
                ACTE(_)
                m_gph[(j, q)] = act.n
            m_q2[j] = dve.n

        def emit_trh(j):
            for q in range(4):
                def _(t, j=j, q=q):
                    act.wait_planned(t, m_gph[(j, q)])
                    sm, vv = pst2_free[0]
                    sm.wait(t, vv)
                    ins = None
                    for jj in range(8):
                        ins = t.transpose(
                            ps_t2[:, jj * 128:(jj + 1) * 128],
                            h1q[:, q * 1024 + jj * 128:q * 1024 + (jj + 1) * 128],
                            ident[:, :])
                    pe.inc(ins)
                PE(_)
                m_trh[(j, q)] = pe.n

                def _(v, j=j, q=q):
                    pe.wait(v, m_trh[(j, q)])
                    ins = v.tensor_copy(x2qT[:, q * 1024:(q + 1) * 1024],
                                        ps_t2[:, :])
                    dve.inc(ins)
                DVE(_)
                pst2_free[0] = (dve, dve.n)
            m_x2qT[j] = dve.n

        def emit_mm2(j):
            def _(t, j=j):
                dve.wait(t, m_x2qT[j])
                if j == 0:
                    sem_wT.wait(t, m_w2T)
                if j >= 1:
                    dve.wait(t, m_r2[j - 1])
                for half in range(2):
                    ins = None
                    for kc in range(32):
                        ins = t.matmul(
                            ps_f[:, half * 512:half * 512 + 512],
                            x2qT[:, kc * 128:(kc + 1) * 128],
                            w2T[:, kc * 1024 + half * 512:
                                kc * 1024 + half * 512 + 512],
                            start=(kc == 0), stop=(kc == 31))
                    pe.inc(ins)
            PE(_)
            m_mm2[j] = pe.n

        def emit_tail_a(j):
            def _(v, j=j):
                pe.wait(v, m_mm2[j])
                if j >= 2:
                    act.wait_planned(v, m_ln2[j - 2])
                if j >= 1:
                    act.wait_planned(v, m_sq2a_d[j - 1])
                b = 8 * (j % 2)
                vb = vb_of(j)
                r2 = r2b[:, (j % 2) * 1024:(j % 2) * 1024 + 1024]
                ins = v.scalar_tensor_tensor(
                    r2, ps_f[:, :], vec2[:, vb + 3:vb + 4],
                    x1b[:, (j % 3) * 1024:(j % 3) * 1024 + 1024],
                    ALU.mult, ALU.add, accum_out=vec2[:, b:b + 1])
                vsync(v, ins)
                ins = v.tensor_scalar_mul(vec2[:, b + 1:b + 2], vec2[:, b:b + 1],
                                          -1.0 / D)
                dve.inc(ins)
            DVE(_)
            m_r2v = dve.n
            m_r2[j] = m_r2v

            def _(sc, j=j, m_r2v=m_r2v):
                dve.wait(sc, m_r2v)
                b = 8 * (j % 2)
                r2 = r2b[:, (j % 2) * 1024:(j % 2) * 1024 + 1024]
                if j == NSLAB - 1:
                    scr = h1[:, 0:1024]
                else:
                    scr = r2b[:, ((j + 1) % 2) * 1024:((j + 1) % 2) * 1024 + 1024]
                ins = sc.activation(scr, r2, AF.Square,
                                    bias=vec2[:, b + 1:b + 2],
                                    accum_out=vec2[:, b + 2:b + 3])
                asyncb(sc, ins)
                ins = sc.activation(vec2[:, b + 3:b + 4], vec2[:, b + 2:b + 3],
                                    AF.Sqrt, scale=1.0 / D, bias=vec2[:, 30:31])
                act.inc(ins)
            ACTE(_)
            m_sq2a = act.n
            m_sq2a_d[j] = act.n

            def _(v, j=j, m_sq2a=m_sq2a):
                act.wait_planned(v, m_sq2a)
                b = 8 * (j % 2)
                ins = v.reciprocal(vec2[:, b + 4:b + 5], vec2[:, b + 3:b + 4])
                vsync(v, ins)
                ins = v.tensor_mul(vec2[:, b + 5:b + 6], vec2[:, b + 1:b + 2],
                                   vec2[:, b + 4:b + 5])
                dve.inc(ins)
            DVE(_)
            m_rstd2[j] = dve.n

        emit_stage1(0)
        emit_stage1_pe(0)
        emit_stage1(1)
        emit_mm1(0, 0)
        emit_act_a(0, 0)
        emit_mm1(0, 1)
        emit_act_a(0, 1)
        emit_q2conv(0)
        for i in range(NSLAB):
            if i + 1 < NSLAB:
                emit_stage1_pe(i + 1)
            if i + 2 < NSLAB:
                emit_stage1(i + 2)
            if i + 1 < NSLAB:
                emit_mm1(i + 1, 0)
                emit_act_a(i + 1, 0)
                emit_mm1(i + 1, 1)
                emit_act_a(i + 1, 1)
                if i >= 1:
                    emit_store(i - 1)
            emit_trh(i)
            emit_mm2(i)
            if i + 1 < NSLAB:
                emit_q2conv(i + 1)
            emit_tail_a(i)

        for jj in (NSLAB - 2, NSLAB - 1):
            def _(sc, jj=jj):
                dve.wait(sc, m_rstd2[jj])
                sem_op[(jj - 1) % 2].wait_all(sc)
                b = 8 * (jj % 2)
                r2 = r2b[:, (jj % 2) * 1024:(jj % 2) * 1024 + 1024]
                ins = sc.activation(o_sb[:, :], r2, AF.Identity,
                                    scale=vec2[:, b + 4:b + 5],
                                    bias=vec2[:, b + 5:b + 6])
                act.inc(ins)
            ACTE(_)
            m_ln2[jj] = act.n
            emit_store(jj)

        if debug:
            def _(s):
                s.wait_ge(sem_op[0].h, 8 * 16)
                s.wait_ge(sem_op[1].h, 8 * 16)
                s.wait_ge(sem_s.h, 16)
                s.dma_start(out=vec_d[:, 0:32], in_=vec[:, :]).then_inc(sem_o.h, 16)
                s.dma_start(out=vec_d[:, 32:64], in_=vec2[:, :]).then_inc(sem_o.h, 16)
                s.dma_start(out=vec_d[0:1, 64:96],
                            in_=sml[0:1, 0:32]).then_inc(sem_o.h, 16)
                s.wait_ge(sem_o.h, 48)
            SYN(_)
            sem_o.n += 48
        def _(s):
            s.wait_ge(sem_op[0].h, 8 * 16)
            s.wait_ge(sem_op[1].h, 8 * 16)
            if debug:
                s.wait_ge(sem_o.h, 48)
        SYN(_)
        if debug:
            sem_s.maxw = max(sem_s.maxw, 16)
        sem_op[0].maxw = max(sem_op[0].maxw, 8 * 16)
        sem_op[1].maxw = max(sem_op[1].maxw, 8 * 16)

        s_2b.close()
        s_mid.close()

        allsems = [sem_g, sem_s, sem_o, pe, act, dve, gp, sem_xb, sem_xT,
                   sem_xTo, sem_wT] + sem_wl + sem_ts + sem_xr + sem_x1s + \
            sem_y + sem_op + sem_wq + sem_kv
        for s in allsems:
            assert s.maxw <= s.n, f"{s.name}: waited {s.maxw} > total {s.n}"

    return nc


_NC = None


def kernel(L0_attn_x, L0_attn_w_qkv, L0_attn_w_out, L0_attn_b_out,
           L0_ffn_w1, L0_ffn_w2):
    global _NC
    x = np.ascontiguousarray(np.asarray(L0_attn_x, dtype=np.float32))
    wqkv = np.ascontiguousarray(np.asarray(L0_attn_w_qkv, dtype=np.float32))
    wout = np.ascontiguousarray(np.asarray(L0_attn_w_out, dtype=np.float32))
    bout = np.ascontiguousarray(np.asarray(L0_attn_b_out, dtype=np.float32))
    w1v = np.ascontiguousarray(np.asarray(L0_ffn_w1, dtype=np.float32))
    w2v = np.ascontiguousarray(np.asarray(L0_ffn_w2, dtype=np.float32))

    if _NC is None:
        _NC = build_program()

    in_maps = []
    for c in range(8):
        b, h = c // 2, c % 2
        xo = np.ascontiguousarray(x[b, h * T_OWN:(h + 1) * T_OWN])
        xt = np.ascontiguousarray(x[b, (1 - h) * T_OWN:(2 - h) * T_OWN])
        in_maps.append(dict(x_own=xo, x_oth=xt, w_qkv=wqkv, w_out=wout,
                            b_out=bout, w1=w1v, w2=w2v))
    res = run_bass_kernel_spmd(_NC, in_maps, core_ids=list(range(8)))
    y = np.empty((B_, SEQ, D), np.float32)
    for c in range(8):
        b, h = c // 2, c % 2
        y[b, h * T_OWN:(h + 1) * T_OWN] = res.results[c]["out"]
    return y


# version 22
# speedup vs baseline: 1.0187x; 1.0112x over previous
import contextlib
import numpy as np

import concourse.bass as bass
import concourse.mybir as mybir
from concourse.bass_utils import run_bass_kernel_spmd

F32 = mybir.dt.float32
BF16 = mybir.dt.bfloat16
I32 = mybir.dt.int32
AF = mybir.ActivationFunctionType
ALU = mybir.AluOpType
AX = mybir.AxisListType

B_, SEQ, D = 4, 4096, 1024
T_OWN = 2048
T_TOT = 4096
HID = 4096
NPAIR = 8
NQB = 4
QB = 512
NKC = 32
NSLAB = 16
EPS_LN = 1e-5
EPS_Q = 1e-5


class Sem:
    def __init__(self, nc, es, name):
        self.h = es.enter_context(nc.semaphore(name))
        self.name = name
        self.n = 0
        self.maxw = 0

    def inc(self, instr, k=1):
        instr.then_inc(self.h, k)
        self.n += k
        return self.n

    def wait(self, eng, val):
        assert val <= self.n, f"{self.name}: forward wait {val} > emitted {self.n}"
        if val <= 0:
            return
        self.maxw = max(self.maxw, val)
        eng.wait_ge(self.h, val)

    def wait_planned(self, eng, val):
        if val <= 0:
            return
        self.maxw = max(self.maxw, val)
        eng.wait_ge(self.h, val)

    def wait_all(self, eng):
        self.wait(eng, self.n)


def build_program(debug=False):
    nc = bass.Bass("TRN2", target_bir_lowering=False, debug=False)

    x_own = nc.dram_tensor("x_own", [T_OWN, D], F32, kind="ExternalInput").ap()
    x_oth = nc.dram_tensor("x_oth", [T_OWN, D], F32, kind="ExternalInput").ap()
    w_qkv = nc.dram_tensor("w_qkv", [D, 3 * D], F32, kind="ExternalInput").ap()
    w_out = nc.dram_tensor("w_out", [D, D], F32, kind="ExternalInput").ap()
    b_out = nc.dram_tensor("b_out", [D], F32, kind="ExternalInput").ap()
    w1 = nc.dram_tensor("w1", [HID, D], F32, kind="ExternalInput").ap()
    w2 = nc.dram_tensor("w2", [D, HID], F32, kind="ExternalInput").ap()
    out = nc.dram_tensor("out", [T_OWN, D], F32, kind="ExternalOutput").ap()

    xbf_d = nc.dram_tensor("xbf_d", [T_TOT, D], BF16, kind="Internal").ap()
    dbg_kind = "ExternalOutput" if debug else "Internal"
    w1t_d = nc.dram_tensor("w1t_d", [HID, D], BF16, kind=dbg_kind).ap()
    w2t_d = nc.dram_tensor("w2t_d", [D, HID], BF16, kind=dbg_kind).ap()
    ho_d = (nc.dram_tensor("ho_d", [128, 8 * T_OWN], BF16,
                           kind="ExternalOutput").ap() if debug else None)
    vec_d = (nc.dram_tensor("vec_d", [128, 96], F32,
                            kind="ExternalOutput").ap() if debug else None)
    x1_d = nc.dram_tensor("x1_d", [T_OWN, D], F32,
                          kind="ExternalOutput" if debug else "Internal").ap()

    m_sc = {}
    m_pv = {}

    es = contextlib.ExitStack()
    s_att2a = contextlib.ExitStack()
    s_att = contextlib.ExitStack()
    s_mid = contextlib.ExitStack()
    s_2a = contextlib.ExitStack()
    s_2b = contextlib.ExitStack()

    with es:
        def SB(stack, name, shape, dt):
            return stack.enter_context(nc.sbuf_tensor(name, shape, dt))

        def PS(stack, name, shape, dt):
            return stack.enter_context(nc.psum_tensor(name, shape, dt))

        ident = SB(es, "ident", [128, 128], BF16)
        ones_bf = SB(es, "ones_bf", [128, 2], BF16)
        ones_fc = SB(es, "ones_fc", [128, 1], F32)
        ones_f = SB(es, "ones_f", [1, 128], F32)
        negones_f = SB(es, "negones_f", [1, 128], F32)
        onesrow_bf = SB(es, "onesrow_bf", [1, 128], BF16)
        b_out_bf = SB(es, "b_out_bf", [1, D], BF16)
        vec = SB(es, "vec", [128, 32], F32)
        vec2 = SB(es, "vec2", [128, 32], F32)
        sml = SB(es, "sml", [1, 64], F32)
        acc_w = SB(es, "acc_w", [128, 4], F32)

        sem_g = Sem(nc, es, "sem_g")
        sem_s = Sem(nc, es, "sem_s")
        sem_o = Sem(nc, es, "sem_o")
        sem_xb = Sem(nc, es, "sem_xb")
        sem_xT = Sem(nc, es, "sem_xT")
        sem_wl = [Sem(nc, es, f"sem_wl{i}") for i in range(2)]
        sem_ts = [Sem(nc, es, f"sem_ts{i}") for i in range(2)]
        sem_wT = Sem(nc, es, "sem_wT")
        sem_xr = [Sem(nc, es, f"sem_xr{i}") for i in range(3)]
        sem_x1s = [Sem(nc, es, f"sem_x1s{i}") for i in range(2)]
        sem_y = [Sem(nc, es, f"sem_y{i}") for i in range(3)]
        sem_op = [Sem(nc, es, f"sem_op{i}") for i in range(2)]
        sem_wq = [Sem(nc, es, f"sem_wq{i}") for i in range(2)]
        sem_kv = [Sem(nc, es, f"sem_kv{i}") for i in range(2)]
        pe = Sem(nc, es, "pe")
        act = Sem(nc, es, "act")
        dve = Sem(nc, es, "dve")
        gp = Sem(nc, es, "gp")

        block = es.enter_context(nc.Block())
        PE = block.tensor
        ACTE = block.scalar
        DVE = block.vector
        GPS = block.gpsimd
        SYN = block.sync

        headoutT = s_att2a.enter_context(
            nc.sbuf_tensor("headoutT", [128, 8 * T_OWN], BF16, side="right"))
        xT = SB(s_att, "xT", [128, 8 * T_TOT], BF16)
        QT = SB(s_att, "QT", [128, 8 * T_OWN], BF16)
        k_pair = [SB(s_att, f"k_pair{i}", [128, T_TOT], BF16) for i in range(2)]
        v_aug = [SB(s_att, f"v_aug{i}", [128, 32 * 130 + 64], BF16) for i in range(2)]
        vfm = [SB(s_att, f"vfm{i}", [128, 512], BF16) for i in range(2)]
        probs = [SB(s_att, f"probs{i}", [128, 1024], BF16) for i in range(2)]
        sqb = [SB(s_att, f"sqb{i}", [128, 512], BF16) for i in range(2)]
        wkv = [SB(s_att, f"wkv{i}", [128, 8 * 256], BF16) for i in range(2)]
        wqm = [SB(s_att, f"wqm{i}", [128, 8 * 128], BF16) for i in range(2)]
        rec = SB(s_att, "rec", [64, 1024], F32)
        den2 = SB(s_att, "den2", [1, 1024], BF16)
        raw = [SB(s_att, f"raw{i}", [128, 512], BF16) for i in range(4)]
        wstage = SB(s_att, "wstage", [128, 1024], F32)
        tern_a = SB(s_att, "tern_a", [128, 1024], BF16)
        tern_b = SB(s_att, "tern_b", [128, 1024], BF16)
        tern_c = SB(s_att, "tern_c", [128, 1024], BF16)

        ps_s = [PS(s_att, f"ps_s{i}", [128, 1024], F32) for i in range(2)]
        ps_oA = PS(s_att, "ps_oA", [128, 512], F32)
        ps_oB = PS(s_att, "ps_oB", [128, 512], F32)
        ps_x = PS(s_att, "ps_x", [128, 512], F32)
        ps_t = PS(s_att, "ps_t", [128, 1024], BF16)

        psx_free = [(dve, 0)]

        def vsync(v, ins):
            dve.inc(ins)
            dve.wait(v, dve.n)

        def asyncb(sc, ins):
            act.inc(ins)
            act.wait_planned(sc, act.n)

        def psx_acquire(eng):
            s, v = psx_free[0]
            s.wait(eng, v)

        pst_free = [(dve, 0)]

        def _(g):
            g.memset(ident[:, :], 0.0)
            g.affine_select(out=ident[:, :], in_=ident[:, :],
                            compare_op=ALU.not_equal, fill=1.0, base=0,
                            pattern=[[-1, 128]], channel_multiplier=1)
            g.memset(ones_bf[:, :], 1.0)
            g.memset(ones_bf[64:128, 1:2], 0.0)
            ins = g.memset(onesrow_bf[:, :], 1.0)
            gp.inc(ins)
            g.dma_start(out=xbf_d[0:T_OWN, :], in_=x_own[:, :]).then_inc(sem_xb.h, 16)
            g.dma_start(out=xbf_d[T_OWN:T_TOT, :], in_=x_oth[:, :]).then_inc(sem_xb.h, 16)
            g.dma_start(out=b_out_bf[0:1, :], in_=b_out[None, :]).then_inc(sem_xb.h, 16)
        GPS(_)
        sem_xb.n += 48
        m_ident = gp.n

        def _(v):
            v.memset(ones_fc[:, :], 1.0)
            v.memset(ones_f[:, :], 1.0)
            v.memset(negones_f[:, :], -1.0)
            v.memset(acc_w[:, :], 0.0)
            v.memset(sml[:, :], 0.0)
            v.memset(vec2[:, :], 0.0)
            v.memset(vec[:, :], 0.0)
            for i in range(2):
                for kt in range(32):
                    v.memset(v_aug[i][:, kt * 130 + 64:kt * 130 + 65], 1.0)
                    v.memset(v_aug[i][:, kt * 130 + 129:kt * 130 + 130], 1.0)
            ins = v.memset(vec2[:, 30:31], EPS_LN)
            dve.inc(ins)
        DVE(_)
        m_prelude = dve.n

        def _(s):
            s.wait_ge(sem_xb.h, 32)
            for c in range(8):
                s.dma_start(out=xT[:, c * T_TOT:(c + 1) * T_TOT],
                            in_=xbf_d[:, c * 128:(c + 1) * 128],
                            transpose=True).then_inc(sem_xT.h, 16)
        SYN(_)
        sem_xT.n += 8 * 16

        m_wq = [0] * 8
        m_wq_pe = [0] * 8

        def emit_wq_load(mm):
            def _(g):
                if mm >= 2:
                    pe.wait(g, m_wq_pe[mm - 2])
                for k in range(8):
                    g.dma_start(out=wqm[mm % 2][:, k * 128:(k + 1) * 128],
                                in_=w_qkv[k * 128:(k + 1) * 128,
                                          mm * 128:(mm + 1) * 128]
                                ).then_inc(sem_wq[mm % 2].h, 16)
            GPS(_)
            sem_wq[mm % 2].n += 128

        emit_wq_load(0)
        emit_wq_load(1)

        m_qt = [0] * 10
        m_qev = {}
        sqb_free = [(pe, 0), (pe, 0)]

        wq_queue = []
        m_wst_load = {}
        m_wst_dve = {}
        m_tern_dve = {}
        m_tern_st = {}

        def w_tile_src(t):
            if t < 32:
                return w1[t * 128:(t + 1) * 128, :]
            tt = t - 32
            return w2[(tt % 8) * 128:(tt % 8) * 128 + 128,
                      (tt // 8) * 1024:(tt // 8) * 1024 + 1024]

        def w_tile_dst(t):
            if t < 32:
                return w1t_d[t * 128:(t + 1) * 128, :]
            tt = t - 32
            return w2t_d[(tt % 8) * 128:(tt % 8) * 128 + 128,
                         (tt // 8) * 1024:(tt // 8) * 1024 + 1024]

        def mk_stats_load(t):
            def emit():
                def _(s):
                    if t >= 1:
                        dve.wait(s, m_wst_dve[t - 1])
                    s.dma_start(out=wstage[:, :],
                                in_=w_tile_src(t)).then_inc(sem_wl[t % 2].h, 16)
                SYN(_)
                sem_wl[t % 2].n += 16
            return emit

        def mk_stats_red(t):
            def emit():
                def _(v):
                    sem_wl[t % 2].wait_all(v)
                    col = 0 if t < 32 else 1
                    ins = v.reduce_sum(acc_w[:, 2:3], wstage[:, :],
                                       axis=AX.X, apply_absolute_value=True)
                    vsync(v, ins)
                    ins = v.tensor_add(acc_w[:, col:col + 1], acc_w[:, col:col + 1],
                                       acc_w[:, 2:3])
                    dve.inc(ins)
                DVE(_)
                m_wst_dve[t] = dve.n
            return emit

        m_thr = [0]

        def mk_s_finalize():
            def emit():
                def _(t):
                    dve.wait(t, m_wst_dve[63])
                    psx_acquire(t)
                    t.matmul(ps_x[0:1, 0:1], acc_w[:, 0:1], ones_fc[:, 0:1],
                             start=True, stop=True)
                    ins = t.matmul(ps_x[0:1, 1:2], acc_w[:, 1:2], ones_fc[:, 0:1],
                                   start=True, stop=True)
                    pe.inc(ins)
                PE(_)
                m1 = pe.n

                def _(v):
                    pe.wait(v, m1)
                    v.tensor_scalar(sml[0:1, 37:38], ps_x[0:1, 0:1],
                                    1.0 / (HID * D), None, ALU.mult)
                    ins = v.tensor_scalar(sml[0:1, 38:39], ps_x[0:1, 1:2],
                                          1.0 / (HID * D), None, ALU.mult)
                    vsync(v, ins)
                    v.tensor_scalar_max(sml[0:1, 37:38], sml[0:1, 37:38], EPS_Q)
                    ins = v.tensor_scalar_max(sml[0:1, 38:39], sml[0:1, 38:39],
                                              EPS_Q)
                    vsync(v, ins)
                    v.tensor_scalar_mul(sml[0:1, 39:40], sml[0:1, 37:38], 0.5)
                    ins = v.tensor_scalar_mul(sml[0:1, 40:41], sml[0:1, 38:39], 0.5)
                    dve.inc(ins)
                DVE(_)
                m2 = dve.n
                psx_free[0] = (dve, m2)

                def _(t):
                    dve.wait(t, m2)
                    psx_acquire(t)
                    t.matmul(ps_x[:, 0:1], ones_f[0:1, :], sml[0:1, 39:40],
                             start=True, stop=True)
                    t.matmul(ps_x[:, 1:2], ones_f[0:1, :], sml[0:1, 40:41],
                             start=True, stop=True)
                    t.matmul(ps_x[:, 2:3], ones_f[0:1, :], sml[0:1, 37:38],
                             start=True, stop=True)
                    ins = t.matmul(ps_x[:, 3:4], ones_f[0:1, :], sml[0:1, 38:39],
                                   start=True, stop=True)
                    pe.inc(ins)
                PE(_)
                m3 = pe.n

                def _(v):
                    pe.wait(v, m3)
                    v.tensor_copy(vec[:, 2:3], ps_x[:, 0:1])
                    v.tensor_scalar_mul(vec[:, 3:4], ps_x[:, 0:1], -1.0)
                    v.tensor_copy(vec[:, 4:5], ps_x[:, 1:2])
                    v.tensor_scalar_mul(vec[:, 5:6], ps_x[:, 1:2], -1.0)
                    v.tensor_copy(vec[:, 6:7], ps_x[:, 2:3])
                    ins = v.tensor_copy(vec[:, 7:8], ps_x[:, 3:4])
                    dve.inc(ins)
                DVE(_)
                psx_free[0] = (dve, dve.n)
                m_thr[0] = dve.n
            return emit

        def mk_tern_load(t):
            def emit():
                def _(s):
                    if t >= 1:
                        dve.wait(s, m_tern_dve[t - 1])
                    else:
                        dve.wait(s, m_wst_dve[63])
                    s.dma_start(out=wstage[:, :],
                                in_=w_tile_src(t)).then_inc(sem_wl[t % 2].h, 16)
                SYN(_)
                sem_wl[t % 2].n += 16
            return emit

        def mk_tern_comp(t):
            def emit():
                def _(v):
                    sem_wl[t % 2].wait_all(v)
                    if t >= 1:
                        sem_ts[(t - 1) % 2].wait_all(v)
                    w = wstage[:, :]
                    thr = vec[:, 2:3] if t < 32 else vec[:, 4:5]
                    nthr = vec[:, 3:4] if t < 32 else vec[:, 5:6]
                    v.tensor_scalar(tern_a[:, :], w, thr, None, ALU.is_ge)
                    ins = v.tensor_scalar(tern_b[:, :], w, nthr, None, ALU.is_le)
                    vsync(v, ins)
                    ins = v.tensor_sub(tern_c[:, 0:1024],
                                       tern_a[:, :], tern_b[:, :])
                    dve.inc(ins)
                DVE(_)
                m_tern_dve[t] = dve.n

                def _(s):
                    dve.wait(s, m_tern_dve[t])
                    s.dma_start(out=w_tile_dst(t),
                                in_=tern_c[:, 0:1024]
                                ).then_inc(sem_ts[t % 2].h, 16)
                SYN(_)
                sem_ts[t % 2].n += 16
            return emit

        for t in range(64):
            wq_queue.append(mk_stats_load(t))
            wq_queue.append(mk_stats_red(t))
        wq_queue.append(mk_s_finalize())
        for t in range(64):
            wq_queue.append(mk_tern_load(t))
            wq_queue.append(mk_tern_comp(t))

        m_kv = [0] * 10
        m_negM = [0] * 10
        m_pair_pe_done = [0] * 10
        m_pair_act_done = [0] * 10
        m_kvproj_done = [0] * 10

        def emit_wkv(p):
            pb = p % 2
            def _(g):
                if p >= 2:
                    pe.wait(g, m_kvproj_done[p - 2])
                for k in range(8):
                    g.dma_start(
                        out=wkv[pb][:, k * 256:k * 256 + 128],
                        in_=w_qkv[k * 128:(k + 1) * 128,
                                  D + p * 128:D + p * 128 + 128]
                    ).then_inc(sem_kv[pb].h, 16)
                    g.dma_start(
                        out=wkv[pb][:, k * 256 + 128:k * 256 + 256],
                        in_=w_qkv[k * 128:(k + 1) * 128,
                                  2 * D + p * 128:2 * D + p * 128 + 128]
                    ).then_inc(sem_kv[pb].h, 16)
            GPS(_)
            sem_kv[pb].n += 256

        def make_prep(p, A0ref):
            ops = []
            pb = p % 2

            def mk_qproj(n):
                def emit():
                    def _(t):
                        if n == 0:
                            sem_wq[p % 2].wait_all(t)
                            if p == 0:
                                sem_xT.wait_all(t)
                                gp.wait(t, m_ident)
                        psx_acquire(t)
                        ins = None
                        for k in range(8):
                            ins = t.matmul(
                                ps_x[:, :],
                                wqm[p % 2][:, k * 128:(k + 1) * 128],
                                xT[:, k * T_TOT + n * 512:
                                   k * T_TOT + n * 512 + 512],
                                start=(k == 0), stop=(k == 7))
                        pe.inc(ins)
                    PE(_)
                    m_mm = pe.n
                    m_wq_pe[p] = pe.n

                    def _(v):
                        pe.wait(v, m_mm)
                        ins = v.tensor_copy(
                            QT[:, p * T_OWN + n * 512:p * T_OWN + n * 512 + 512],
                            ps_x[:, :])
                        dve.inc(ins)
                    DVE(_)
                    psx_free[0] = (dve, dve.n)
                    m_qev[(p, n)] = dve.n
                    if n == 3:
                        m_qt[p] = dve.n
                return emit
            for n in range(4):
                ops.append(mk_qproj(n))

            def em_wqload():
                if p + 2 < NPAIR:
                    emit_wq_load(p + 2)
            ops.append(em_wqload)

            def mk_qnorm(j):
                def emit():
                    def _(g):
                        dve.wait(g, m_qev[(p, j)])
                        s_, v_ = sqb_free[j % 2]
                        s_.wait(g, v_)
                        sl = QT[:, p * T_OWN + j * 512:p * T_OWN + j * 512 + 512]
                        ins = g.tensor_tensor(sqb[j % 2][:, :], sl, sl, ALU.mult)
                        gp.inc(ins)
                    GPS(_)
                    m_sq = gp.n

                    def _(t):
                        gp.wait(t, m_sq)
                        psx_acquire(t)
                        t.matmul(ps_x[0:1, :], ones_bf[0:64, 0:1],
                                 sqb[j % 2][0:64, :], start=True, stop=True,
                                 tile_position=(0, 0))
                        ins = t.matmul(ps_x[32:33, :], ones_bf[64:128, 0:1],
                                       sqb[j % 2][64:128, :],
                                       start=True, stop=True,
                                       tile_position=(64, 32))
                        pe.inc(ins)
                    PE(_)
                    m_nm = pe.n
                    sqb_free[j % 2] = (pe, pe.n)

                    def _(v):
                        pe.wait(v, m_nm)
                        v.reduce_max(sml[0:1, 48 + j:49 + j], ps_x[0:1, :],
                                     axis=AX.X)
                        ins = v.reduce_max(sml[0:1, 52 + j:53 + j],
                                           ps_x[32:33, :], axis=AX.X)
                        dve.inc(ins)
                    DVE(_)
                    psx_free[0] = (dve, dve.n)
                return emit
            for j in range(4):
                ops.append(mk_qnorm(j))

            def em_qfin():
                def _(v):
                    dve.wait(v, dve.n)
                    v.reduce_max(sml[0:1, 2 * p:2 * p + 1], sml[0:1, 48:52],
                                 axis=AX.X)
                    ins = v.reduce_max(sml[0:1, 2 * p + 1:2 * p + 2],
                                       sml[0:1, 52:56], axis=AX.X)
                    dve.inc(ins)
                DVE(_)
            ops.append(em_qfin)

            def em_wkv():
                def _(g):
                    if p >= 2:
                        pe.wait(g, m_pair_pe_done[p - 2])
                    for k in range(8):
                        g.dma_start(
                            out=wkv[pb][:, k * 256:k * 256 + 128],
                            in_=w_qkv[k * 128:(k + 1) * 128,
                                      D + p * 128:D + p * 128 + 128]
                        ).then_inc(sem_kv[pb].h, 16)
                        g.dma_start(
                            out=wkv[pb][:, k * 256 + 128:k * 256 + 256],
                            in_=w_qkv[k * 128:(k + 1) * 128,
                                      2 * D + p * 128:2 * D + p * 128 + 128]
                        ).then_inc(sem_kv[pb].h, 16)
                GPS(_)
                sem_kv[pb].n += 256

            def mk_kchunk(n):
                def emit():
                    def _(t):
                        sem_kv[pb].wait_all(t)
                        if p < 2 and n == 0:
                            sem_xT.wait_all(t)
                        psx_acquire(t)
                        ins = None
                        for k in range(8):
                            ins = t.matmul(
                                ps_x[:, :],
                                wkv[pb][:, k * 256:k * 256 + 128],
                                xT[:, k * T_TOT + n * 512:k * T_TOT + n * 512 + 512],
                                start=(k == 0), stop=(k == 7))
                        pe.inc(ins)
                    PE(_)
                    m_mm = pe.n

                    def _(v):
                        pe.wait(v, m_mm)
                        if p >= 2 and n == 0:
                            pe.wait(v, m_pair_pe_done[p - 2])
                        ins = v.tensor_copy(
                            k_pair[pb][:, n * 512:(n + 1) * 512], ps_x[:, :])
                        dve.inc(ins)
                    DVE(_)
                    psx_free[0] = (dve, dve.n)
                    mk_done[n] = dve.n
                return emit
            mk_done = {}

            def mk_vchunk(n):
                st = {}

                def emit1():
                    def _(t):
                        psx_acquire(t)
                        ins = None
                        for k in range(8):
                            ins = t.matmul(
                                ps_x[:, :],
                                wkv[pb][:, k * 256 + 128:k * 256 + 256],
                                xT[:, k * T_TOT + n * 512:k * T_TOT + n * 512 + 512],
                                start=(k == 0), stop=(k == 7))
                        pe.inc(ins)
                    PE(_)
                    m_mm = pe.n

                    def _(v):
                        pe.wait(v, m_mm)
                        ins = v.tensor_copy(vfm[n % 2][:, :], ps_x[:, :])
                        dve.inc(ins)
                    DVE(_)
                    st["ev"] = dve.n
                    psx_free[0] = (dve, dve.n)

                def emit2():
                    def _(t):
                        dve.wait(t, st["ev"])
                        sm, vv = pst_free[0]
                        sm.wait(t, vv)
                        ins = None
                        for j in range(4):
                            ins = t.transpose(ps_t[:, j * 128:(j + 1) * 128],
                                              vfm[n % 2][:, j * 128:(j + 1) * 128],
                                              ident[:, :])
                        pe.inc(ins)
                    PE(_)
                    m_tr = pe.n

                    def _(v):
                        pe.wait(v, m_tr)
                        if p >= 2 and n == 0:
                            pe.wait(v, m_pair_pe_done[p - 2])
                        ins = None
                        for j in range(4):
                            kt = n * 4 + j
                            v.tensor_copy(
                                v_aug[pb][:, kt * 130:kt * 130 + 64],
                                ps_t[:, j * 128:j * 128 + 64])
                            ins = v.tensor_copy(
                                v_aug[pb][:, kt * 130 + 65:kt * 130 + 129],
                                ps_t[:, j * 128 + 64:j * 128 + 128])
                        dve.inc(ins)
                    DVE(_)
                    pst_free[0] = (dve, dve.n)
                    if n == 7:
                        m_kv[p] = dve.n
                        m_kvproj_done[p] = pe.n
                return [emit1, emit2]
            for n in range(8):
                ops.append(mk_kchunk(n))
                for c in mk_vchunk(n):
                    ops.append(c)

            mk_nm = {}

            def mk_knorm(n):
                def emit():
                    def _(g):
                        dve.wait(g, mk_done[n])
                        s_, v_ = sqb_free[n % 2]
                        s_.wait(g, v_)
                        sl = k_pair[pb][:, n * 512:(n + 1) * 512]
                        ins = g.tensor_tensor(sqb[n % 2][:, :], sl, sl, ALU.mult)
                        gp.inc(ins)
                    GPS(_)
                    m_sq = gp.n

                    def _(t):
                        gp.wait(t, m_sq)
                        psx_acquire(t)
                        t.matmul(ps_x[0:1, :], ones_bf[0:64, 0:1],
                                 sqb[n % 2][0:64, :], start=True, stop=True,
                                 tile_position=(0, 0))
                        ins = t.matmul(ps_x[32:33, :], ones_bf[64:128, 0:1],
                                       sqb[n % 2][64:128, :], start=True, stop=True,
                                       tile_position=(64, 32))
                        pe.inc(ins)
                    PE(_)
                    m_nm = pe.n
                    sqb_free[n % 2] = (pe, pe.n)

                    def _(v):
                        pe.wait(v, m_nm)
                        v.reduce_max(sml[0:1, 16 + n:17 + n], ps_x[0:1, :],
                                     axis=AX.X)
                        ins = v.reduce_max(sml[0:1, 24 + n:25 + n], ps_x[32:33, :],
                                           axis=AX.X)
                        dve.inc(ins)
                    DVE(_)
                    psx_free[0] = (dve, dve.n)
                return emit
            for n in range(8):
                ops.append(mk_knorm(n))

            def em_negM():
                def _(v):
                    dve.wait(v, dve.n)
                    v.reduce_max(sml[0:1, 32:33], sml[0:1, 16:24], axis=AX.X)
                    ins = v.reduce_max(sml[0:1, 33:34], sml[0:1, 24:32], axis=AX.X)
                    vsync(v, ins)
                    v.tensor_tensor(sml[0:1, 34:35], sml[0:1, 32:33],
                                    sml[0:1, 2 * p:2 * p + 1], ALU.add)
                    ins = v.tensor_tensor(sml[0:1, 35:36], sml[0:1, 33:34],
                                          sml[0:1, 2 * p + 1:2 * p + 2], ALU.add)
                    vsync(v, ins)
                    ins = v.tensor_tensor(sml[0:1, 36:37], sml[0:1, 34:35],
                                          sml[0:1, 35:36], ALU.max)
                    dve.inc(ins)
                DVE(_)
                m_t = dve.n

                def _(t):
                    dve.wait(t, m_t)
                    psx_acquire(t)
                    ins = t.matmul(ps_x[:, 0:1], negones_f[0:1, :],
                                   sml[0:1, 36:37], start=True, stop=True)
                    pe.inc(ins)
                PE(_)
                m_b = pe.n

                def _(v):
                    pe.wait(v, m_b)
                    if p >= 2:
                        act.wait_planned(v, m_pair_act_done[p - 2])
                    ins = v.tensor_scalar(vec[:, pb:pb + 1], ps_x[:, 0:1],
                                          1.0 / 16.0, None, ALU.mult)
                    dve.inc(ins)
                DVE(_)
                psx_free[0] = (dve, dve.n)
                m_negM[p] = dve.n
            ops.append(em_negM)
            return ops

        emit_wkv(0)
        emit_wkv(1)
        for emitfn in make_prep(0, 0):
            emitfn()
        prep_queue = list(make_prep(1, 0))

        A0 = act.n

        def pexp(G):
            return A0 + G + 1 + 2 * (G // NKC)

        def act_no(p_, qb_):
            Qg = p_ * NQB + qb_ + 1
            return A0 + NKC * Qg + 2 * Qg

        m_no = {}
        m_norm_done = {}
        raw_free = [(dve, 0), (dve, 0)]
        m_bc_prev = [0]
        norm_queue = []
        norms_emitted = [0]
        exp_qblks = []

        for p in range(NPAIR):
            pb = p % 2
            if p + 2 < NPAIR:
                emit_wkv(p + 2)
            for qb in range(NQB):
                gbase = (p * NQB + qb) * NKC
                for kc0 in range(0, NKC, 2):
                    def _(t, p=p, qb=qb, kc0=kc0, pb=pb):
                        for kc in (kc0, kc0 + 1):
                            G = (p * NQB + qb) * NKC + kc
                            sb_ = G % 2
                            if kc == 0:
                                dve.wait(t, m_kv[p])
                                if qb == 0:
                                    dve.wait(t, m_qt[p])
                            if G >= 2:
                                act.wait_planned(t, pexp(G - 2))
                            t.matmul(ps_s[sb_][:, 0:512],
                                     k_pair[pb][0:64, kc * 128:(kc + 1) * 128],
                                     QT[0:64, p * T_OWN + qb * 512:
                                        p * T_OWN + qb * 512 + 512],
                                     start=True, stop=True, tile_position=(0, 0))
                            ins = t.matmul(ps_s[sb_][:, 512:1024],
                                           k_pair[pb][64:128,
                                                      kc * 128:(kc + 1) * 128],
                                           QT[64:128, p * T_OWN + qb * 512:
                                              p * T_OWN + qb * 512 + 512],
                                           start=True, stop=True,
                                           tile_position=(64, 0))
                            pe.inc(ins)
                            m_sc[G] = pe.n
                            if kc >= 1:
                                Gp = G - 1
                                kcp = Gp % NKC
                                spb = Gp % 2
                                if kcp == 0:
                                    if qb > 0:
                                        act.wait_planned(t, act_no(p, qb - 1))
                                    elif p >= 1:
                                        act.wait_planned(t, act_no(p - 1,
                                                                   NQB - 1))
                                act.wait_planned(t, pexp(Gp))
                                t.matmul(ps_oA[:, :],
                                         v_aug[pb][:, kcp * 130:kcp * 130 + 128],
                                         probs[spb][:, 0:512],
                                         start=(kcp == 0), stop=False)
                                ins = t.matmul(ps_oB[:, :],
                                               v_aug[pb][:, kcp * 130 + 65:
                                                          kcp * 130 + 193],
                                               probs[spb][:, 512:1024],
                                               start=(kcp == 0), stop=False)
                                pe.inc(ins)
                                m_pv[Gp] = pe.n
                    PE(_)
                    budget = 2
                    while budget > 0 and prep_queue:
                        prep_queue.pop(0)()
                        budget -= 1
                    if len(norm_queue) >= 2:
                        norm_queue.pop(0)()
                    elif budget > 0 and wq_queue:
                        wq_queue.pop(0)()

                def _(t, p=p, qb=qb, pb=pb):
                    G = (p * NQB + qb) * NKC + (NKC - 1)
                    kcp = NKC - 1
                    act.wait_planned(t, pexp(G))
                    t.matmul(ps_oA[:, :], v_aug[pb][:, kcp * 130:kcp * 130 + 128],
                             probs[G % 2][:, 0:512], start=False, stop=True)
                    ins = t.matmul(ps_oB[:, :],
                                   v_aug[pb][:, kcp * 130 + 65:kcp * 130 + 193],
                                   probs[G % 2][:, 512:1024], start=False,
                                   stop=True)
                    pe.inc(ins)
                    m_pv[G] = pe.n
                PE(_)
                if qb == NQB - 1:
                    m_pair_pe_done[p] = pe.n
                    m_pair_act_done[p] = act_no(p, NQB - 1)

                rb = (2 * (p * NQB + qb)) % 4
                gq = p * NQB + qb
                assert gq < 2 or norms_emitted[0] >= gq - 1, \
                    (gq, norms_emitted[0])
                rfs, rfv = raw_free[rb // 2]
                exp_qblks.append((p, qb, pb, rb, rfs, rfv))
                if qb == NQB - 1:
                    def _(sc, batch=list(exp_qblks)):
                        first = True
                        for (pp, qq, ppb, rb_, rfs_, rfv_) in batch:
                            for kc in range(NKC):
                                G = (pp * NQB + qq) * NKC + kc
                                sb_ = G % 2
                                if first:
                                    dve.wait(sc, m_negM[pp])
                                    first = False
                                pe.wait(sc, m_sc[G])
                                if G >= 2 and (G - 2) in m_pv:
                                    pe.wait(sc, m_pv[G - 2])
                                ins = sc.activation(probs[sb_][:, :],
                                                    ps_s[sb_][:, :], AF.Exp,
                                                    bias=vec[:, ppb:ppb + 1],
                                                    scale=0.125)
                                act.inc(ins)
                            pe.wait(sc, m_pv[(pp * NQB + qq) * NKC + NKC - 1])
                            rfs_.wait_planned(sc, rfv_)
                            ins = sc.activation(raw[rb_][:, :], ps_oA[:, :],
                                                AF.Copy)
                            act.inc(ins)
                            ins = sc.activation(raw[rb_ + 1][:, :], ps_oB[:, :],
                                                AF.Copy)
                            act.inc(ins)
                            assert act.n == act_no(pp, qq), (act.n,
                                                             act_no(pp, qq))
                    ACTE(_)
                    exp_qblks.clear()

                def mk_norm(p=p, qb=qb, rb=rb):
                    def emit():
                        def _(v):
                            act.wait_planned(v, act_no(p, qb))
                            pe.wait(v, m_bc_prev[0])
                            v.tensor_copy(den2[0:1, 0:512], raw[rb][64:65, :])
                            ins = v.tensor_copy(den2[0:1, 512:1024],
                                                raw[rb + 1][64:65, :])
                            dve.inc(ins)
                        DVE(_)
                        m_dc = dve.n

                        def _(t):
                            dve.wait(t, m_dc)
                            psx_acquire(t)
                            t.matmul(ps_x[0:64, :], onesrow_bf[0:1, 0:64],
                                     den2[0:1, 0:512], start=True, stop=True,
                                     tile_position=(0, 0))
                            ins = t.matmul(ps_x[64:128, :], onesrow_bf[0:1, 0:64],
                                           den2[0:1, 512:1024], start=True,
                                           stop=True, tile_position=(0, 64))
                            pe.inc(ins)
                        PE(_)
                        m_bc = pe.n
                        m_bc_prev[0] = m_bc

                        def _(v):
                            pe.wait(v, m_bc)
                            v.reciprocal(rec[0:64, 0:512], ps_x[0:64, :])
                            ins = v.reciprocal(rec[0:64, 512:1024],
                                               ps_x[64:128, :])
                            vsync(v, ins)
                            col = p * T_OWN + qb * 512
                            v.tensor_tensor(headoutT[0:64, col:col + 512],
                                            raw[rb][0:64, :], rec[0:64, 0:512],
                                            ALU.mult)
                            ins = v.tensor_tensor(headoutT[64:128, col:col + 512],
                                                  raw[rb + 1][0:64, :],
                                                  rec[0:64, 512:1024], ALU.mult)
                            dve.inc(ins)
                        DVE(_)
                        psx_free[0] = (dve, dve.n)
                        raw_free[rb // 2] = (dve, dve.n)
                        m_norm_done[(p, qb)] = dve.n
                        norms_emitted[0] += 1
                    return emit
                norm_queue.append(mk_norm())

            while prep_queue:
                prep_queue.pop(0)()
            if p + 2 < NPAIR:
                prep_queue = list(make_prep(p + 2, A0))

        while norm_queue:
            norm_queue.pop(0)()
        while wq_queue:
            wq_queue.pop(0)()

        m_attn_pe = pe.n
        m_attn_dve = dve.n
        m_attn_act = act.n
        m_headout = max(m_norm_done.values())

        if debug:
            def _(s):
                dve.wait(s, m_headout)
                s.dma_start(out=ho_d[:, :], in_=headoutT[:, :]).then_inc(sem_s.h, 16)
            SYN(_)
            sem_s.n += 16

        s_att.close()

        w1T = SB(s_mid, "w1T", [128, 8 * 4096], BF16)
        wout_sb = SB(s_2a, "wout_sb", [128, 8 * 1024], BF16)
        x_res = SB(s_2a, "x_res", [128, 3 * 1024], F32)
        r1b = SB(s_2a, "r1b", [128, 2 * 1024], F32)
        sq1 = SB(s_2a, "sq1", [128, 1024], F32)
        x1st = SB(s_2a, "x1st", [128, 2 * 1024], F32)
        ps_a = [PS(s_2a, f"ps_a{i}", [128, 1024], F32) for i in range(2)]

        def _(g):
            pe.wait(g, m_attn_pe)
            dve.wait(g, m_attn_dve)
            act.wait_planned(g, m_attn_act)
            sem_ts[0].wait_all(g)
            sem_ts[1].wait_all(g)
            for c in range(8):
                g.dma_start(out=wout_sb[:, c * 1024:(c + 1) * 1024],
                            in_=w_out[c * 128:(c + 1) * 128, :]).then_inc(sem_g.h, 16)
        GPS(_)
        sem_g.n += 128
        m_wout = sem_g.n

        def _(s):
            pe.wait(s, m_attn_pe)
            dve.wait(s, m_attn_dve)
            sem_ts[0].wait_all(s)
            sem_ts[1].wait_all(s)
            for c in range(8):
                s.dma_start(out=w1T[:, c * 4096:(c + 1) * 4096],
                            in_=w1t_d[:, c * 128:(c + 1) * 128],
                            transpose=True).then_inc(sem_wT.h, 16)
        SYN(_)
        sem_wT.n += 128
        m_w1T = sem_wT.n

        m_r1_dve = [0] * NSLAB
        m_ln1_act = [0] * NSLAB
        m_rstd1 = [0] * NSLAB

        def emit_ln1_norm(sc, s_i):
            dve.wait(sc, m_rstd1[s_i])
            if s_i >= 2:
                sem_x1s[s_i % 2].wait_all(sc)
            b = 8 * (s_i % 2)
            r1 = r1b[:, (s_i % 2) * 1024:(s_i % 2) * 1024 + 1024]
            ins = sc.activation(
                x1st[:, (s_i % 2) * 1024:(s_i % 2) * 1024 + 1024], r1,
                AF.Identity, scale=vec2[:, b + 4:b + 5],
                bias=vec2[:, b + 5:b + 6])
            act.inc(ins)
            m_ln1_act[s_i] = act.n

        def emit_x1_store(s_i):
            def _(s, s_i=s_i):
                act.wait_planned(s, m_ln1_act[s_i])
                s.dma_start(out=x1_d[s_i * 128:(s_i + 1) * 128, :],
                            in_=x1st[:, (s_i % 2) * 1024:(s_i % 2) * 1024 + 1024]
                            ).then_inc(sem_x1s[s_i % 2].h, 16)
            SYN(_)
            sem_x1s[s_i % 2].n += 16

        for s_i in range(NSLAB):
            def _(s, s_i=s_i):
                if s_i == 0:
                    pe.wait(s, m_attn_pe)
                    dve.wait(s, m_attn_dve)
                if s_i >= 3:
                    dve.wait(s, m_r1_dve[s_i - 3])
                s.dma_start(out=x_res[:, (s_i % 3) * 1024:(s_i % 3) * 1024 + 1024],
                            in_=x_own[s_i * 128:(s_i + 1) * 128, :]
                            ).then_inc(sem_xr[s_i % 3].h, 16)
            SYN(_)
            sem_xr[s_i % 3].n += 16

            def _(t, s_i=s_i):
                if s_i == 0:
                    sem_g.wait(t, m_wout)
                    sem_xb.wait(t, 48)
                    dve.wait(t, m_headout)
                if s_i >= 2:
                    dve.wait(t, m_r1_dve[s_i - 2])
                pa = ps_a[s_i % 2]
                for half in range(2):
                    for c in range(8):
                        t.matmul(
                            pa[:, half * 512:half * 512 + 512],
                            headoutT[:, c * T_OWN + s_i * 128:
                                     c * T_OWN + s_i * 128 + 128],
                            wout_sb[:, c * 1024 + half * 512:
                                    c * 1024 + half * 512 + 512],
                            start=(c == 0), stop=False)
                    ins = t.matmul(pa[:, half * 512:half * 512 + 512],
                                   onesrow_bf[0:1, :],
                                   b_out_bf[0:1, half * 512:half * 512 + 512],
                                   start=False, stop=True)
                    pe.inc(ins)
            PE(_)
            m_op = pe.n

            def _(v, s_i=s_i, m_op=m_op):
                pe.wait(v, m_op)
                sem_xr[s_i % 3].wait_all(v)
                if s_i >= 2:
                    act.wait_planned(v, m_ln1_act[s_i - 2])
                b = 8 * (s_i % 2)
                r1 = r1b[:, (s_i % 2) * 1024:(s_i % 2) * 1024 + 1024]
                ins = v.scalar_tensor_tensor(
                    r1, ps_a[s_i % 2][:, :], 1.0,
                    x_res[:, (s_i % 3) * 1024:(s_i % 3) * 1024 + 1024],
                    ALU.mult, ALU.add, accum_out=vec2[:, b:b + 1])
                vsync(v, ins)
                ins = v.tensor_scalar_mul(vec2[:, b + 1:b + 2], vec2[:, b:b + 1],
                                          -1.0 / D)
                dve.inc(ins)
            DVE(_)
            m_r1 = dve.n
            m_r1_dve[s_i] = m_r1

            def _(sc, s_i=s_i, m_r1=m_r1):
                if s_i >= 1:
                    emit_ln1_norm(sc, s_i - 1)
                dve.wait(sc, m_r1)
                b = 8 * (s_i % 2)
                r1 = r1b[:, (s_i % 2) * 1024:(s_i % 2) * 1024 + 1024]
                ins = sc.activation(sq1[:, :], r1, AF.Square,
                                    bias=vec2[:, b + 1:b + 2],
                                    accum_out=vec2[:, b + 2:b + 3])
                asyncb(sc, ins)
                ins = sc.activation(vec2[:, b + 3:b + 4], vec2[:, b + 2:b + 3],
                                    AF.Sqrt, scale=1.0 / D, bias=vec2[:, 30:31])
                act.inc(ins)
            ACTE(_)
            m_sqa = act.n
            if s_i >= 1:
                emit_x1_store(s_i - 1)

            def _(v, s_i=s_i, m_sqa=m_sqa):
                act.wait_planned(v, m_sqa)
                b = 8 * (s_i % 2)
                ins = v.reciprocal(vec2[:, b + 4:b + 5], vec2[:, b + 3:b + 4])
                vsync(v, ins)
                ins = v.tensor_mul(vec2[:, b + 5:b + 6], vec2[:, b + 1:b + 2],
                                   vec2[:, b + 4:b + 5])
                dve.inc(ins)
            DVE(_)
            m_rstd1[s_i] = dve.n

        def _(sc):
            emit_ln1_norm(sc, NSLAB - 1)
        ACTE(_)
        emit_x1_store(NSLAB - 1)

        m_2a_pe = pe.n
        m_2a_dve = dve.n
        m_2a_act = act.n

        s_att2a.close()
        s_2a.close()

        w2T = SB(s_2b, "w2T", [128, 32 * 1024], BF16)
        x1b = SB(s_2b, "x1b", [128, 3 * 1024], F32)
        h1 = SB(s_2b, "h1", [128, 4096], F32)
        h1q = SB(s_2b, "h1q", [128, 4096], BF16)
        x1q = SB(s_2b, "x1q", [128, 1024], BF16)
        x1qT = SB(s_2b, "x1qT", [128, 2 * 1024], BF16)
        x2qT = SB(s_2b, "x2qT", [128, 4096], BF16)
        i32a = SB(s_2b, "i32a", [128, 1024], I32)
        i32b = SB(s_2b, "i32b", [128, 2 * 1024], I32)
        r2b = SB(s_2b, "r2b", [128, 2 * 1024], F32)
        o_sb = SB(s_2b, "o_sb", [128, 1024], F32)
        ps_h = [PS(s_2b, f"ps_h{i}", [128, 1024], F32) for i in range(2)]
        ps_f = PS(s_2b, "ps_f", [128, 1024], F32)
        ps_t2 = PS(s_2b, "ps_t2", [128, 1024], BF16)

        def _(s):
            pe.wait(s, m_2a_pe)
            dve.wait(s, m_2a_dve)
            act.wait_planned(s, m_2a_act)
            sem_ts[0].wait_all(s)
            sem_ts[1].wait_all(s)
            sem_x1s[0].wait_all(s)
            sem_x1s[1].wait_all(s)
            for c in range(32):
                s.dma_start(out=w2T[:, c * 1024:(c + 1) * 1024],
                            in_=w2t_d[:, c * 128:(c + 1) * 128],
                            transpose=True).then_inc(sem_wT.h, 16)
        SYN(_)
        sem_wT.n += 32 * 16
        m_w2T = sem_wT.n

        pst2_free = [(dve, 0)]
        m_q1v = {}
        m_gpx = {}
        m_tr1 = {}
        m_x1qT = {}
        m_gelu = {}
        m_q2 = {}
        m_q2q = {}
        m_gph = {}
        m_trh = {}
        m_x2qT = {}
        m_mm2 = {}
        m_mm1d = {}
        m_r2 = {}
        m_ln2 = {}
        m_rstd2 = {}
        m_sq2a_d = {}

        def vb_of(j):
            return 16 + 6 * (j % 2)

        def emit_stage1(j):
            def _(s, j=j):
                if j == 0:
                    pe.wait(s, m_2a_pe)
                    dve.wait(s, m_2a_dve)
                    act.wait_planned(s, m_2a_act)
                    sem_x1s[0].wait_all(s)
                    sem_x1s[1].wait_all(s)
                if j >= 3:
                    dve.wait(s, m_r2[j - 3])
                s.dma_start(out=x1b[:, (j % 3) * 1024:(j % 3) * 1024 + 1024],
                            in_=x1_d[j * 128:(j + 1) * 128, :]
                            ).then_inc(sem_y[j % 3].h, 16)
            SYN(_)
            sem_y[j % 3].n += 16

            def _(v, j=j):
                sem_y[j % 3].wait_all(v)
                if j >= 1:
                    gp.wait(v, m_gpx[j - 1])
                if j >= 2:
                    act.wait_planned(v, m_gelu[(j - 2, 3)])
                vb = vb_of(j)
                x1 = x1b[:, (j % 3) * 1024:(j % 3) * 1024 + 1024]
                ins = v.reduce_max(vec2[:, 28:29], x1, axis=AX.X,
                                   apply_absolute_value=True)
                vsync(v, ins)
                ins = v.tensor_scalar_max(vec2[:, 28:29], vec2[:, 28:29], EPS_Q)
                vsync(v, ins)
                ins = v.reciprocal(vec2[:, 29:30], vec2[:, 28:29])
                vsync(v, ins)
                v.tensor_scalar_mul(vec2[:, vb:vb + 1], vec2[:, 29:30], 127.0)
                ins = v.tensor_scalar_mul(vec2[:, 28:29], vec2[:, 28:29],
                                          1.0 / 127.0)
                vsync(v, ins)
                ins = v.tensor_mul(vec2[:, vb + 1:vb + 2], vec2[:, 28:29],
                                   vec[:, 6:7])
                vsync(v, ins)
                ins = v.tensor_scalar(i32a[:, :], x1, vec2[:, vb:vb + 1], None,
                                      ALU.mult)
                dve.inc(ins)
            DVE(_)
            m_q1v[j] = dve.n

            def _(g, j=j):
                dve.wait(g, m_q1v[j])
                if j >= 1:
                    pe.wait(g, m_tr1[j - 1])
                ins = g.tensor_copy(x1q[:, :], i32a[:, :])
                gp.inc(ins)
            GPS(_)
            m_gpx[j] = gp.n


        def emit_stage1_pe(j):
            def _(t, j=j):
                gp.wait(t, m_gpx[j])
                sm, vv = pst2_free[0]
                sm.wait(t, vv)
                ins = None
                for jj in range(8):
                    ins = t.transpose(ps_t2[:, jj * 128:(jj + 1) * 128],
                                      x1q[:, jj * 128:(jj + 1) * 128], ident[:, :])
                pe.inc(ins)
            PE(_)
            m_tr1[j] = pe.n

            def _(v, j=j):
                pe.wait(v, m_tr1[j])
                ins = v.tensor_copy(x1qT[:, (j % 2) * 1024:(j % 2) * 1024 + 1024],
                                    ps_t2[:, :])
                dve.inc(ins)
            DVE(_)
            m_x1qT[j] = dve.n
            pst2_free[0] = (dve, dve.n)

        def emit_mm1(j, part):
            for q in (0, 1) if part == 0 else (2, 3):
                def _(t, j=j, q=q):
                    if q == 0:
                        dve.wait(t, m_x1qT[j])
                        if j == 0:
                            sem_wT.wait(t, m_w1T)
                    ga = m_gelu.get((j, q - 2), m_gelu.get((j - 1, q + 2), 0))
                    act.wait_planned(t, ga)
                    ph = ps_h[q % 2]
                    xoff = (j % 2) * 1024
                    for half in range(2):
                        ins = None
                        for k in range(8):
                            ins = t.matmul(
                                ph[:, half * 512:half * 512 + 512],
                                x1qT[:, xoff + k * 128:xoff + (k + 1) * 128],
                                w1T[:, k * 4096 + q * 1024 + half * 512:
                                    k * 4096 + q * 1024 + half * 512 + 512],
                                start=(k == 0), stop=(k == 7))
                        pe.inc(ins)
                PE(_)
                m_mm1d[(j, q)] = pe.n

        def emit_act_a(j, part):
            def _(sc, j=j, part=part):
                if part == 0 and j >= 2 and (j - 2) in m_rstd2:
                    jj = j - 2
                    dve.wait(sc, m_rstd2[jj])
                    if jj >= 1:
                        sem_op[(jj - 1) % 2].wait_all(sc)
                    b = 8 * (jj % 2)
                    r2 = r2b[:, (jj % 2) * 1024:(jj % 2) * 1024 + 1024]
                    ins = sc.activation(o_sb[:, :], r2, AF.Identity,
                                        scale=vec2[:, b + 4:b + 5],
                                        bias=vec2[:, b + 5:b + 6])
                    act.inc(ins)
                    m_ln2[jj] = act.n
                for q in (0, 1) if part == 0 else (2, 3):
                    pe.wait(sc, m_mm1d[(j, q)])
                    if q == 0:
                        dve.wait(sc, m_q1v[j])
                        if j >= 1:
                            dve.wait(sc, m_q2[j - 1])
                    vb = vb_of(j)
                    ins = sc.activation(h1[:, q * 1024:(q + 1) * 1024],
                                        ps_h[q % 2][:, :], AF.Gelu,
                                        scale=vec2[:, vb + 1:vb + 2])
                    act.inc(ins)
                    m_gelu[(j, q)] = act.n
            ACTE(_)

        def emit_store(j):
            def _(s, j=j):
                act.wait_planned(s, m_ln2[j])
                s.dma_start(out=out[j * 128:(j + 1) * 128, :],
                            in_=o_sb[:, :]).then_inc(sem_op[j % 2].h, 16)
            SYN(_)
            sem_op[j % 2].n += 16

        def emit_q2conv(j):
            def _(v, j=j):
                act.wait_planned(v, m_gelu[(j, 3)])
                vb = vb_of(j)
                ins = v.reduce_max(vec2[:, 28:29], h1[:, :], axis=AX.X,
                                   apply_absolute_value=True)
                vsync(v, ins)
                ins = v.tensor_scalar_max(vec2[:, 28:29], vec2[:, 28:29], EPS_Q)
                vsync(v, ins)
                ins = v.reciprocal(vec2[:, 29:30], vec2[:, 28:29])
                vsync(v, ins)
                v.tensor_scalar_mul(vec2[:, vb + 2:vb + 3], vec2[:, 29:30], 127.0)
                ins = v.tensor_scalar_mul(vec2[:, 28:29], vec2[:, 28:29],
                                          1.0 / 127.0)
                vsync(v, ins)
                ins = v.tensor_mul(vec2[:, vb + 3:vb + 4], vec2[:, 28:29],
                                   vec[:, 7:8])
                dve.inc(ins)
            DVE(_)

            for q in range(4):
                def _(v, j=j, q=q):
                    if q < 2:
                        cg = m_gph.get((j - 1, q + 2), 0)
                    else:
                        cg = m_gph[(j, q - 2)]
                    act.wait_planned(v, cg)
                    vb = vb_of(j)
                    ins = v.tensor_scalar(
                        i32b[:, (q % 2) * 1024:(q % 2) * 1024 + 1024],
                        h1[:, q * 1024:(q + 1) * 1024],
                        vec2[:, vb + 2:vb + 3], None, ALU.mult)
                    dve.inc(ins)
                DVE(_)
                m_q2q[(j, q)] = dve.n

                def _(sc, j=j, q=q):
                    dve.wait(sc, m_q2q[(j, q)])
                    tg = m_trh.get((j - 1, q), 0)
                    pe.wait(sc, tg)
                    ins = sc.activation(
                        h1q[:, q * 1024:(q + 1) * 1024],
                        i32b[:, (q % 2) * 1024:(q % 2) * 1024 + 1024], AF.Copy)
                    act.inc(ins)
                ACTE(_)
                m_gph[(j, q)] = act.n
            m_q2[j] = dve.n

        def emit_trh(j):
            for q in range(4):
                def _(t, j=j, q=q):
                    act.wait_planned(t, m_gph[(j, q)])
                    sm, vv = pst2_free[0]
                    sm.wait(t, vv)
                    ins = None
                    for jj in range(8):
                        ins = t.transpose(
                            ps_t2[:, jj * 128:(jj + 1) * 128],
                            h1q[:, q * 1024 + jj * 128:q * 1024 + (jj + 1) * 128],
                            ident[:, :])
                    pe.inc(ins)
                PE(_)
                m_trh[(j, q)] = pe.n

                def _(v, j=j, q=q):
                    pe.wait(v, m_trh[(j, q)])
                    ins = v.tensor_copy(x2qT[:, q * 1024:(q + 1) * 1024],
                                        ps_t2[:, :])
                    dve.inc(ins)
                DVE(_)
                pst2_free[0] = (dve, dve.n)
            m_x2qT[j] = dve.n

        def emit_mm2(j):
            def _(t, j=j):
                dve.wait(t, m_x2qT[j])
                if j == 0:
                    sem_wT.wait(t, m_w2T)
                if j >= 1:
                    dve.wait(t, m_r2[j - 1])
                for half in range(2):
                    ins = None
                    for kc in range(32):
                        ins = t.matmul(
                            ps_f[:, half * 512:half * 512 + 512],
                            x2qT[:, kc * 128:(kc + 1) * 128],
                            w2T[:, kc * 1024 + half * 512:
                                kc * 1024 + half * 512 + 512],
                            start=(kc == 0), stop=(kc == 31))
                    pe.inc(ins)
            PE(_)
            m_mm2[j] = pe.n

        def emit_tail_a(j):
            def _(v, j=j):
                pe.wait(v, m_mm2[j])
                if j >= 2:
                    act.wait_planned(v, m_ln2[j - 2])
                if j >= 1:
                    act.wait_planned(v, m_sq2a_d[j - 1])
                b = 8 * (j % 2)
                vb = vb_of(j)
                r2 = r2b[:, (j % 2) * 1024:(j % 2) * 1024 + 1024]
                ins = v.scalar_tensor_tensor(
                    r2, ps_f[:, :], vec2[:, vb + 3:vb + 4],
                    x1b[:, (j % 3) * 1024:(j % 3) * 1024 + 1024],
                    ALU.mult, ALU.add, accum_out=vec2[:, b:b + 1])
                vsync(v, ins)
                ins = v.tensor_scalar_mul(vec2[:, b + 1:b + 2], vec2[:, b:b + 1],
                                          -1.0 / D)
                dve.inc(ins)
            DVE(_)
            m_r2v = dve.n
            m_r2[j] = m_r2v

            def _(sc, j=j, m_r2v=m_r2v):
                dve.wait(sc, m_r2v)
                b = 8 * (j % 2)
                r2 = r2b[:, (j % 2) * 1024:(j % 2) * 1024 + 1024]
                if j == NSLAB - 1:
                    scr = h1[:, 0:1024]
                else:
                    scr = r2b[:, ((j + 1) % 2) * 1024:((j + 1) % 2) * 1024 + 1024]
                ins = sc.activation(scr, r2, AF.Square,
                                    bias=vec2[:, b + 1:b + 2],
                                    accum_out=vec2[:, b + 2:b + 3])
                asyncb(sc, ins)
                ins = sc.activation(vec2[:, b + 3:b + 4], vec2[:, b + 2:b + 3],
                                    AF.Sqrt, scale=1.0 / D, bias=vec2[:, 30:31])
                act.inc(ins)
            ACTE(_)
            m_sq2a = act.n
            m_sq2a_d[j] = act.n

            def _(v, j=j, m_sq2a=m_sq2a):
                act.wait_planned(v, m_sq2a)
                b = 8 * (j % 2)
                ins = v.reciprocal(vec2[:, b + 4:b + 5], vec2[:, b + 3:b + 4])
                vsync(v, ins)
                ins = v.tensor_mul(vec2[:, b + 5:b + 6], vec2[:, b + 1:b + 2],
                                   vec2[:, b + 4:b + 5])
                dve.inc(ins)
            DVE(_)
            m_rstd2[j] = dve.n

        emit_stage1(0)
        emit_stage1_pe(0)
        emit_stage1(1)
        emit_mm1(0, 0)
        emit_act_a(0, 0)
        emit_mm1(0, 1)
        emit_act_a(0, 1)
        emit_q2conv(0)
        for i in range(NSLAB):
            if i + 1 < NSLAB:
                emit_stage1_pe(i + 1)
            if i + 2 < NSLAB:
                emit_stage1(i + 2)
            if i + 1 < NSLAB:
                emit_mm1(i + 1, 0)
                emit_act_a(i + 1, 0)
                emit_mm1(i + 1, 1)
                emit_act_a(i + 1, 1)
                if i >= 1:
                    emit_store(i - 1)
            emit_trh(i)
            emit_mm2(i)
            if i + 1 < NSLAB:
                emit_q2conv(i + 1)
            emit_tail_a(i)

        for jj in (NSLAB - 2, NSLAB - 1):
            def _(sc, jj=jj):
                dve.wait(sc, m_rstd2[jj])
                sem_op[(jj - 1) % 2].wait_all(sc)
                b = 8 * (jj % 2)
                r2 = r2b[:, (jj % 2) * 1024:(jj % 2) * 1024 + 1024]
                ins = sc.activation(o_sb[:, :], r2, AF.Identity,
                                    scale=vec2[:, b + 4:b + 5],
                                    bias=vec2[:, b + 5:b + 6])
                act.inc(ins)
            ACTE(_)
            m_ln2[jj] = act.n
            emit_store(jj)

        if debug:
            def _(s):
                s.wait_ge(sem_op[0].h, 8 * 16)
                s.wait_ge(sem_op[1].h, 8 * 16)
                s.wait_ge(sem_s.h, 16)
                s.dma_start(out=vec_d[:, 0:32], in_=vec[:, :]).then_inc(sem_o.h, 16)
                s.dma_start(out=vec_d[:, 32:64], in_=vec2[:, :]).then_inc(sem_o.h, 16)
                s.dma_start(out=vec_d[0:1, 64:96],
                            in_=sml[0:1, 0:32]).then_inc(sem_o.h, 16)
                s.wait_ge(sem_o.h, 48)
            SYN(_)
            sem_o.n += 48
        def _(s):
            s.wait_ge(sem_op[0].h, 8 * 16)
            s.wait_ge(sem_op[1].h, 8 * 16)
            if debug:
                s.wait_ge(sem_o.h, 48)
        SYN(_)
        if debug:
            sem_s.maxw = max(sem_s.maxw, 16)
        sem_op[0].maxw = max(sem_op[0].maxw, 8 * 16)
        sem_op[1].maxw = max(sem_op[1].maxw, 8 * 16)

        s_2b.close()
        s_mid.close()

        allsems = [sem_g, sem_s, sem_o, pe, act, dve, gp, sem_xb, sem_xT,
                   sem_wT] + sem_wl + sem_ts + sem_xr + sem_x1s + sem_y + \
            sem_op + sem_wq + sem_kv
        for s in allsems:
            assert s.maxw <= s.n, f"{s.name}: waited {s.maxw} > total {s.n}"

    return nc


_NC = None


def kernel(L0_attn_x, L0_attn_w_qkv, L0_attn_w_out, L0_attn_b_out,
           L0_ffn_w1, L0_ffn_w2):
    global _NC
    x = np.ascontiguousarray(np.asarray(L0_attn_x, dtype=np.float32))
    wqkv = np.ascontiguousarray(np.asarray(L0_attn_w_qkv, dtype=np.float32))
    wout = np.ascontiguousarray(np.asarray(L0_attn_w_out, dtype=np.float32))
    bout = np.ascontiguousarray(np.asarray(L0_attn_b_out, dtype=np.float32))
    w1v = np.ascontiguousarray(np.asarray(L0_ffn_w1, dtype=np.float32))
    w2v = np.ascontiguousarray(np.asarray(L0_ffn_w2, dtype=np.float32))

    if _NC is None:
        _NC = build_program()

    in_maps = []
    for c in range(8):
        b, h = c // 2, c % 2
        xo = np.ascontiguousarray(x[b, h * T_OWN:(h + 1) * T_OWN])
        xt = np.ascontiguousarray(x[b, (1 - h) * T_OWN:(2 - h) * T_OWN])
        in_maps.append(dict(x_own=xo, x_oth=xt, w_qkv=wqkv, w_out=wout,
                            b_out=bout, w1=w1v, w2=w2v))
    res = run_bass_kernel_spmd(_NC, in_maps, core_ids=list(range(8)))
    y = np.empty((B_, SEQ, D), np.float32)
    for c in range(8):
        b, h = c // 2, c % 2
        y[b, h * T_OWN:(h + 1) * T_OWN] = res.results[c]["out"]
    return y
